# version 13
# baseline (speedup 1.0000x reference)
import numpy as np
import ml_dtypes
import concourse.bass as bass
import concourse.mybir as mybir
from concourse.bass_utils import run_bass_kernel_spmd
from contextlib import ExitStack

F32 = mybir.dt.float32
BF16 = mybir.dt.bfloat16
F16 = mybir.dt.float16
I32 = mybir.dt.int32
I16 = mybir.dt.int16
AF = mybir.ActivationFunctionType
OP = mybir.AluOpType
AX = mybir.AxisListType.X

D = 2048
CT = 16
TPC = 4096
G = 4
TG = 1024
NT = 8
I_FF = 5632
GU_OSL = 1408
DN_HSL = 256
EPS1 = 1e-5
EPS2 = 1e-6
INV_D = 1.0 / D
DEBUG_TAPS = False
DEBUG_TAPS_MID = False




class Sem:
    def __init__(self, h):
        self.h = h
        self.v = 0

    def inc(self, n=1):
        self.v += n
        return self.v


def build_program():
    nc = bass.Bass("TRN2", target_bir_lowering=False)

    def inp(name, shape, dt):
        return nc.dram_tensor(name, shape, dt, kind="ExternalInput").ap()

    x_in = inp("x_in", [TPC, D], F32)
    xprev = inp("xprev", [1, D], F32)
    wrT = inp("wrT", [D, D], F32)
    wkT = inp("wkT", [D, D], F32)
    wvT = inp("wvT", [D, D], F32)
    woT = inp("woT", [D, D], F32)
    tmix = inp("tmix", [128, 3 * CT], F32)
    tdv = inp("tdv", [128, CT], F32)
    tfv = inp("tfv", [128, CT], F32)
    steps8 = inp("steps8", [1, 8], F32)
    mask8 = inp("mask8", [1, 8], F32)
    idbf = inp("idbf", [128, 128], BF16)
    masks_gu = inp("masks_gu", [128, 2 * GU_OSL], I32)
    gup_s = inp("gup_s", [256, GU_OSL], I32)
    gu_sc = inp("gu_sc", [256, 2 * GU_OSL], F16)
    dp_s = inp("dp_s", [704, DN_HSL], I32)
    d_sc = inp("d_sc", [704, 2 * DN_HSL], F16)

    y_out = nc.dram_tensor("y_out", [TPC, D], F32, kind="ExternalOutput").ap()

    def idram(name, shape, dt):
        return nc.dram_tensor(name, shape, dt).ap()

    wbf = idram("wbf", [D, D], BF16)
    wbfT = idram("wbfT", [3 * CT * 128, D], BF16)
    deq_gu_s = idram("deq_gu_s", [D, GU_OSL], BF16)
    deq_dn_s = idram("deq_dn_s", [I_FF, DN_HSL], BF16)
    deq_gu_loc = idram("deq_gu_loc", [GU_OSL, D], BF16)
    deq_gu_all = idram("deq_gu_all", [8 * GU_OSL, D], BF16)
    deq_dn_loc = idram("deq_dn_loc", [DN_HSL, I_FF], BF16)
    deq_dn_all = idram("deq_dn_all", [8 * DN_HSL, I_FF], BF16)
    f_loc = idram("f_loc", [128, CT], F32)
    f_all = idram("f_all", [8 * 128, CT], F32)
    sp_P = idram("sp_P", [D, TPC], BF16)
    sp_M = idram("sp_M", [D, TPC], BF16)
    sp_x2 = idram("sp_x2", [TPC, D], F32)
    if DEBUG_TAPS and DEBUG_TAPS_MID:
        dbg_hs = nc.dram_tensor("dbg_hs", [128, 16400], BF16, kind="ExternalOutput").ap()
        dbg_xr = nc.dram_tensor("dbg_xr", [128, 16384], BF16, kind="ExternalOutput").ap()
        dbg_st = nc.dram_tensor("dbg_st", [128, 640], F32, kind="ExternalOutput").ap()
        dbg_kv = nc.dram_tensor("dbg_kv", [128, 16384], BF16, kind="ExternalOutput").ap()

    es = ExitStack()
    with es:
        sbuf = lambda n, s, d: es.enter_context(nc.sbuf_tensor(n, s, d))
        smallF = sbuf("smallF", [128, 640], F32)
        smallB = sbuf("smallB", [128, 256], BF16)
        _o = [0]

        def sm(w):
            a = _o[0]; _o[0] += w
            return smallF[:, a:a + w]

        ones_c = sm(1)
        eps1_c = sm(1)
        eps2_c = sm(1)
        tdv_sb = sm(CT)
        tfv_sb = sm(CT)
        nexp_sb = sm(CT)
        wdec_sb = sm(CT)
        w1024_sb = sm(CT)
        tmix_sb = sm(3 * CT)
        pball = sm(G * CT)
        steps_b = sm(8)
        mask_b = sm(8)
        fall_sb = sm(128)
        sin_sb = sm(CT)
        sbound = sm(CT)
        c8a = sm(8)
        c8b = sm(8)
        stat_sum = sm(NT)
        stat_sq = sm(NT)
        stat_m = sm(NT)
        stat_r = sm(NT)
        stat_t = sm(NT)
        gapc = sm(1)
        stat_u = sm(NT)
        stat_v = sm(NT)
        stat_w = sm(NT)
        stat_x = sm(NT)
        stat_y = sm(NT)
        _row = sm(144)
        ones_row = _row[0:1, 0:128]
        rows8 = _row[0:1, 128:144]
        prevhs = smallB[:, 0:CT]
        idb_sb = smallB[:, 16:144]
        A_scr = sbuf("A_scr", [128, 9218], F32)
        A_cm = sbuf("A_cm", [128, 16400], BF16)
        A_xM = sbuf("A_xM", [128, 32768], BF16)
        A_big = sbuf("A_big", [128, 26624], BF16)
        A_pan = sbuf("A_pan", [128, 8192], BF16)
        xprev_sb = A_big[0:1, 0:4096].bitcast(F32)
        hprev_sb = A_big[0:1, 4096:6144]
        psF = [es.enter_context(nc.psum_tensor(f"ps{i}", [128, 1024], F32)) for i in range(3)]
        psT = [es.enter_context(nc.psum_tensor(f"psT{i}", [128, 1024], BF16)) for i in range(2)]

        s_ld = Sem(es.enter_context(nc.semaphore("s_ld")))
        s_gld = Sem(es.enter_context(nc.semaphore("s_gld")))
        s_te = Sem(es.enter_context(nc.semaphore("s_te")))
        s_ve = Sem(es.enter_context(nc.semaphore("s_ve")))
        s_se = Sem(es.enter_context(nc.semaphore("s_se")))
        s_sp = Sem(es.enter_context(nc.semaphore("s_sp")))
        s_ag = [Sem(es.enter_context(nc.semaphore(f"s_ag{i}"))) for i in range(3)]
        s_x = [Sem(es.enter_context(nc.semaphore(f"s_x{i}"))) for i in range(2)]
        s_p = [Sem(es.enter_context(nc.semaphore(f"s_p{i}"))) for i in range(2)]
        s_q = [Sem(es.enter_context(nc.semaphore(f"s_q{i}"))) for i in range(2)]
        s_b = [Sem(es.enter_context(nc.semaphore(f"s_b{i}"))) for i in range(2)]
        s_out = Sem(es.enter_context(nc.semaphore("s_out")))
        block = es.enter_context(nc.Block())

        hs3 = A_cm[:, 0:CT * 1025].rearrange("p (c t) -> p c t", t=1025)

        def cmcol(ct, a, b):
            return A_cm[:, ct * 1025 + a: ct * 1025 + b]

        xM_slot = [A_xM[:, 0:CT * 1024], A_xM[:, CT * 1024:2 * CT * 1024]]
        xM3 = [s.rearrange("p (c t) -> p c t", t=1024) for s in xM_slot]
        wkv3 = xM3[0]
        x2ln3 = xM3[1]
        xresF = A_xM[:, 0:4096].bitcast(F32)
        yrowF = A_xM[:, 4096:6144]
        t1rowF = A_xM[:, 6144:10240].bitcast(F32)

        kv3 = A_big[:, 0:CT * 1024].rearrange("p (c t) -> p c t", t=1024)
        ktmp = A_scr[:, 7170:7682].bitcast(BF16)
        hsrow = A_big[:, 16384:18432]
        xrow_a = [A_big[:, 18432 + i * 4096:18432 + (i + 1) * 4096].bitcast(F32) for i in range(2)]
        woT3 = A_big[:, 0:CT * 1024].rearrange("p (c t) -> p c t", t=1024)
        h3 = A_big[:, 0:22 * 1024].rearrange("p (c t) -> p c t", t=1024)

        orow_full = A_cm[:, 0:16384]
        x2row = A_big[:, 16384:20480].bitcast(F32)
        x2lnrow = A_big[:, 20480:22528]
        xresB = A_big[:, 22528:26624].bitcast(F32)
        y23 = A_cm[:, 0:CT * 1024].rearrange("p (c t) -> p c t", t=1024)

        S_buf = A_scr[:, 0:1025]
        wp_buf = A_scr[:, 1025:2050]
        kvf32 = A_scr[:, 2050:3074]
        dmix = A_scr[:, 3074:4098]
        wt_f = dmix
        junk = A_scr[:, 4098:6146]
        zrow = A_scr[:, 6146:7170]
        sgrow = A_scr[:, 7170:7682].bitcast(BF16)
        prowA = [A_scr[:, 7682:8194].bitcast(BF16),
                 A_scr[:, 8194:8706].bitcast(BF16)]
        mrowA = [A_scr[:, 8706:9218].bitcast(BF16),
                 A_scr[:, 1025:1537].bitcast(BF16)]

        panW = [A_pan[:, 0:2048].rearrange("p (c t) -> p c t", t=128),
                A_pan[:, 2048:4096].rearrange("p (c t) -> p c t", t=128)]
        panU = [A_pan[:, 4096:6144].rearrange("p (c t) -> p c t", t=128),
                A_pan[:, 6144:8192].rearrange("p (c t) -> p c t", t=128)]
        panD = [A_pan[:, 0:2816].rearrange("p (c t) -> p c t", t=128),
                A_pan[:, 2816:5632].rearrange("p (c t) -> p c t", t=128)]
        mrowB = [A_pan[:, 0:1024], A_pan[:, 1024:2048]]
        prowB = [A_pan[:, 2048:3072], A_pan[:, 3072:4096]]

        @block.sync
        def _(s):
            for dst, src in [(tmix_sb[:, :], tmix), (tdv_sb[:, :], tdv),
                             (tfv_sb[:, :], tfv), (idb_sb[:, :], idbf),
                             (rows8[:, 0:8], steps8), (rows8[:, 8:16], mask8),
                             (xprev_sb[:, :], xprev)]:
                s.dma_start(out=dst, in_=src).then_inc(s_ld.h, 16)
                s_ld.inc(16)
        ld_consts = s_ld.v

        @block.vector
        def _(v):
            v.memset(ones_c[:, :], 1.0)
            v.memset(eps1_c[:, :], EPS1)
            v.memset(eps2_c[:, :], EPS2)
            v.memset(zrow[:, :], 0.0)
            v.memset(ones_row[:, :], 1.0)
            v.memset(pball[:, 0:CT], 1.0)
            v.memset(sbound[:, :], 0.0).then_inc(s_ve.h, 1)
        s_ve.inc(1)

        @block.scalar
        def _(sc):
            sc.wait_ge(s_ld.h, ld_consts)
            sc.activation(out=nexp_sb[:, :], in_=tdv_sb[:, :], func=AF.Exp
                          ).then_inc(s_se.h, 1)
        s_se.inc(1)

        @block.vector
        def _(v):
            v.wait_ge(s_se.h, s_se.v)
            v.tensor_scalar(out=nexp_sb[:, :], in0=nexp_sb[:, :], scalar1=-1.0,
                            scalar2=None, op0=OP.mult).then_inc(s_ve.h, 1)
        s_ve.inc(1)

        @block.scalar
        def _(sc):
            sc.wait_ge(s_ve.h, s_ve.v)
            sc.activation(out=wdec_sb[:, :], in_=nexp_sb[:, :], func=AF.Exp)
            sc.activation(out=w1024_sb[:, :], in_=nexp_sb[:, :], func=AF.Exp,
                          scale=1024.0).then_inc(s_se.h, 1)
        s_se.inc(1)

        @block.vector
        def _(v):
            v.wait_ge(s_se.h, s_se.v)
            for g in range(1, G):
                v.tensor_tensor(out=pball[:, g * CT:(g + 1) * CT],
                                in0=pball[:, (g - 1) * CT:g * CT],
                                in1=w1024_sb[:, :], op=OP.mult)
                v.tensor_copy(out=gapc[:, :], in_=ones_c[:, :])
            v.tensor_copy(out=gapc[:, :], in_=ones_c[:, :]).then_inc(s_ve.h, 1)
        s_ve.inc(1)
        const_ve = s_ve.v

        @block.tensor
        def _(t):
            t.wait_ge(s_ld.h, ld_consts)
            t.wait_ge(s_ve.h, const_ve)
            t.matmul(psF[0][:, 0:8], ones_row[0:1, :], rows8[0:1, 0:8],
                     start=True, stop=True)
            t.matmul(psF[0][:, 8:16], ones_row[0:1, :], rows8[0:1, 8:16],
                     start=True, stop=True).then_inc(s_te.h, 1)
        s_te.inc(1)

        @block.vector
        def _(v):
            v.wait_ge(s_te.h, s_te.v)
            v.tensor_copy(out=steps_b[:, :], in_=psF[0][:, 0:8])
            v.tensor_copy(out=mask_b[:, :], in_=psF[0][:, 8:16])
            v.reduce_sum(out=stat_sum[0:1, 0:1], in_=xprev_sb[:, :], axis=AX
                         ).then_inc(s_ve.h, 1)
        s_ve.inc(1)
        bc_ve = s_ve.v

        @block.scalar
        def _(sc):
            sc.wait_ge(s_ve.h, bc_ve)
            sc.activation(out=junk[0:1, :], in_=xprev_sb[:, :], func=AF.Square,
                          accum_out=stat_sq[0:1, 0:1]).then_inc(s_se.h, 1)
        s_se.inc(1)

        @block.vector
        def _(v):
            v.wait_ge(s_se.h, s_se.v)
            v.tensor_scalar(out=stat_t[0:1, 0:1], in0=stat_sq[0:1, 0:1],
                            scalar1=INV_D, scalar2=None, op0=OP.mult)
            v.tensor_scalar(out=stat_m[0:1, 0:1], in0=stat_sum[0:1, 0:1],
                            scalar1=INV_D, scalar2=None, op0=OP.mult)
            v.tensor_copy(out=gapc[:, :], in_=ones_c[:, :])
            v.scalar_tensor_tensor(out=stat_r[0:1, 0:1], in0=stat_m[0:1, 0:1],
                                   scalar=stat_m[0:1, 0:1], in1=stat_t[0:1, 0:1],
                                   op0=OP.mult, op1=OP.subtract).then_inc(s_ve.h, 1)
        s_ve.inc(1)

        @block.scalar
        def _(sc):
            sc.wait_ge(s_ve.h, s_ve.v)
            sc.activation(out=stat_r[0:1, 0:1], in_=stat_r[0:1, 0:1], func=AF.Ln,
                          scale=-1.0, bias=eps1_c[0:1, :])
            sc.activation(out=stat_r[0:1, 0:1], in_=stat_r[0:1, 0:1], func=AF.Exp,
                          scale=-0.5).then_inc(s_se.h, 1)
        s_se.inc(1)

        @block.vector
        def _(v):
            v.wait_ge(s_se.h, s_se.v)
            v.tensor_scalar(out=hprev_sb[:, :], in0=xprev_sb[:, :],
                            scalar1=stat_m[0:1, 0:1], scalar2=stat_r[0:1, 0:1],
                            op0=OP.subtract, op1=OP.mult).then_inc(s_ve.h, 1)
        s_ve.inc(1)
        hp_ve = s_ve.v

        @block.tensor
        def _(t):
            t.wait_ge(s_ve.h, hp_ve)
            for ct in range(CT):
                ins = t.transpose(psT[0][:, 2 * ct:2 * ct + 1],
                                  hprev_sb[0:1, ct * 128:(ct + 1) * 128],
                                  idb_sb[0:1, 0:1])
            ins.then_inc(s_te.h, 1)
        s_te.inc(1)

        @block.vector
        def _(v):
            v.wait_ge(s_te.h, s_te.v)
            v.tensor_copy(out=prevhs[:, :], in_=psT[0][:, 0:2 * CT:2]).then_inc(s_ve.h, 1)
        s_ve.inc(1)
        prevhs_ve = s_ve.v

        w32 = [A_big[:, i * 8192:(i + 1) * 8192].bitcast(F32) for i in range(2)]
        wbf_sb = [A_big[:, 16384 + i * 4096:16384 + (i + 1) * 4096] for i in range(2)]
        wsrcs = [wrT, wkT, wvT, woT]
        cast_ld = []
        cast_ve = []
        cast_sp = []
        n_cast = 4 * CT // 2
        for i in range(n_cast):
            m, kt2 = divmod(i, CT // 2)
            kt = 2 * kt2
            sl = i % 2

            @block.sync
            def _(s, m=m, kt=kt, sl=sl, i=i):
                if i < 2:
                    s.wait_ge(s_ve.h, prevhs_ve)
                else:
                    s.wait_ge(s_ve.h, cast_ve[i - 2])
                srcv = wsrcs[m][128 * kt:128 * (kt + 2), :].rearrange(
                    "(k p) o -> p k o", p=128)
                s.dma_start(out=w32[sl][:, :].rearrange("p (k o) -> p k o", o=D),
                            in_=srcv).then_inc(s_x[sl].h, 16)
            cast_ld.append(s_x[sl].inc(16))

            @block.vector
            def _(v, sl=sl, i=i):
                v.wait_ge(s_x[sl].h, cast_ld[i])
                if i >= 2:
                    v.wait_ge(s_q[sl].h, cast_sp[i - 2])
                v.tensor_copy(out=wbf_sb[sl][:, :], in_=w32[sl][:, :]).then_inc(s_ve.h, 1)
            cast_ve.append(s_ve.inc(1))

            @block.scalar
            def _(s, m=m, kt=kt, sl=sl, i=i):
                s.wait_ge(s_ve.h, cast_ve[i])
                if m < 3:
                    for j in range(2):
                        dst = wbfT[m * CT * 128:(m + 1) * CT * 128,
                                   128 * (kt + j):128 * (kt + j + 1)]
                        dst3 = dst.rearrange("(t k) o -> k t o", k=128)
                        src3 = wbf_sb[sl][:, j * D:(j + 1) * D].rearrange(
                            "p (t o) -> p t o", o=128)
                        s.dma_start(out=dst3, in_=src3).then_inc(s_q[sl].h, 16)
                    s_qinc = 32
                else:
                    s.dma_start(out=wbf[128 * kt:128 * (kt + 2), :].rearrange(
                                    "(k p) o -> p k o", p=128),
                                in_=wbf_sb[sl][:, :].rearrange("p (k o) -> p k o", o=D)
                                ).then_inc(s_q[sl].h, 16)
                    s_qinc = 16
            cast_sp.append(s_q[sl].inc(32 if m < 3 else 16))

        u16 = A_big[:, 0:2816].bitcast(I16)
        us16p = [A_big[:, 2816:5632].bitcast(I16), A_big[:, 5632:8448].bitcast(I16)]
        t216p = [A_big[:, 8448:11264].bitcast(I16), A_big[:, 11264:14080].bitcast(I16)]
        mag16 = A_big[:, 14080:16896]
        relu1p_b = A_big[:, 16896:19712]
        mk_sb = A_xM[:, 0:5632].bitcast(I32)
        msign = mk_sb[:, 0:GU_OSL].bitcast(I16)
        mexp = mk_sb[:, GU_OSL:2 * GU_OSL].bitcast(I16)
        sc32b = A_xM[:, 5632:11264].bitcast(F32)
        relu1p = [A_xM[:, 11264:14080], relu1p_b]
        pk2 = [A_xM[:, 14080:16896].bitcast(I32), A_xM[:, 16896:19712].bitcast(I32)]
        sc2 = [A_xM[:, 19712:22528].bitcast(F16), A_xM[:, 22528:25344].bitcast(F16)]
        evb = [A_xM[:, 25344:26752], A_xM[:, 28160:29568]]
        odb = [A_xM[:, 26752:28160], A_xM[:, 29568:30976]]

        @block.sync
        def _(s):
            s.dma_start(out=mk_sb[:, :], in_=masks_gu).then_inc(s_ld.h, 16)
        s_ld.inc(16)

        deq_tiles = []
        for a in range(2):
            deq_tiles.append((gup_s[128 * a:128 * (a + 1), :],
                              gu_sc[128 * a:128 * (a + 1), :], 128, GU_OSL, ("gu", a)))
        for a in range(6):
            npart = 128 if a < 5 else 64
            deq_tiles.append((dp_s[128 * a:128 * a + npart, :],
                              d_sc[128 * a:128 * a + npart, :], npart, DN_HSL, ("dn", a)))
        gu8 = deq_gu_s.rearrange("(k e) o -> k e o", e=8)
        dn8 = deq_dn_s.rearrange("(k e) o -> k e o", e=8)

        tile_ld = [0] * len(deq_tiles)
        tile_ve = [0] * len(deq_tiles)
        pl_store = {}
        n_jp_done = 0
        for ti, (src_pk, src_sc, n_part, w_words, tag) in enumerate(deq_tiles):
            ts_ = ti % 2
            w16 = 2 * w_words

            @block.sync
            def _(s, ti=ti, ts_=ts_, src_pk=src_pk, src_sc=src_sc,
                  n_part=n_part, w_words=w_words, w16=w16):
                if ti >= 2:
                    s.wait_ge(s_ve.h, tile_ve[ti - 2])
                s.dma_start(out=pk2[ts_][0:n_part, 0:w_words], in_=src_pk
                            ).then_inc(s_x[ts_].h, 16)
                s.dma_start(out=sc2[ts_][0:n_part, 0:w16], in_=src_sc
                            ).then_inc(s_x[ts_].h, 16)
            tile_ld[ti] = s_x[ts_].inc(32)

            @block.vector
            def _(v, ti=ti, ts_=ts_, n_part=n_part, w16=w16):
                v.wait_ge(s_x[ts_].h, tile_ld[ti])
                if ti == 0:
                    v.wait_ge(s_ld.h, s_ld.v)
                v.tensor_copy(out=sc32b[0:n_part, 0:w16], in_=sc2[ts_][0:n_part, 0:w16]
                              ).then_inc(s_ve.h, 1)
            s_ve.inc(1)

            kind, a = tag

            def emitA(jp, jpar, n_part=n_part, w_words=w_words, w16=w16, ts_=ts_):
                pk16 = pk2[ts_][0:n_part, 0:w_words].bitcast(I16)

                @block.vector
                def _(v):
                    sl = 12 - 4 * jp
                    if sl > 0:
                        v.tensor_scalar(out=u16[0:n_part, 0:w16], in0=pk16, scalar1=sl,
                                        scalar2=None, op0=OP.logical_shift_left)
                    else:
                        v.tensor_copy(out=u16[0:n_part, 0:w16], in_=pk16)
                    v.tensor_tensor(out=us16p[jpar][0:n_part, 0:w16],
                                    in0=u16[0:n_part, 0:w16],
                                    in1=msign[0:n_part, 0:w16], op=OP.bitwise_and)
                    v.tensor_scalar(out=t216p[jpar][0:n_part, 0:w16],
                                    in0=u16[0:n_part, 0:w16],
                                    scalar1=6, scalar2=None, op0=OP.logical_shift_right)
                    v.tensor_tensor(out=t216p[jpar][0:n_part, 0:w16],
                                    in0=t216p[jpar][0:n_part, 0:w16],
                                    in1=mexp[0:n_part, 0:w16], op=OP.bitwise_and)
                    v.tensor_scalar(out=t216p[jpar][0:n_part, 0:w16],
                                    in0=t216p[jpar][0:n_part, 0:w16],
                                    scalar1=16128.0, scalar2=None, op0=OP.add
                                    ).then_inc(s_ve.h, 1)
                t2_ve = s_ve.inc(1)

                @block.scalar
                def _(sc):
                    sc.wait_ge(s_ve.h, t2_ve)
                    sc.activation(out=relu1p[jpar][0:n_part, 0:w16],
                                  in_=t216p[jpar][0:n_part, 0:w16].bitcast(BF16),
                                  func=AF.Relu, scale=-1.0,
                                  bias=ones_c[0:n_part, :]).then_inc(s_se.h, 1)
                return s_se.inc(1)

            def emitB(jp, jpar, r_se, n_part=n_part, w_words=w_words, w16=w16,
                      kind=kind, a=a):
                @block.vector
                def _(v):
                    v.wait_ge(s_se.h, r_se)
                    if (jpar, "st") in pl_store and pl_store[(jpar, "st")] > 0:
                        v.wait_ge(s_q[jpar].h, pl_store[(jpar, "st")])
                    v.tensor_tensor(out=mag16[0:n_part, 0:w16],
                                    in0=t216p[jpar][0:n_part, 0:w16].bitcast(BF16),
                                    in1=relu1p[jpar][0:n_part, 0:w16], op=OP.subtract)
                    v.tensor_tensor(out=mag16[0:n_part, 0:w16].bitcast(I16),
                                    in0=mag16[0:n_part, 0:w16].bitcast(I16),
                                    in1=us16p[jpar][0:n_part, 0:w16], op=OP.bitwise_or)
                    v.tensor_tensor(out=evb[jpar][0:n_part, 0:w_words],
                                    in0=mag16[0:n_part, 0:w16][:, 0::2],
                                    in1=sc32b[0:n_part, 0:w16][:, 0::2], op=OP.mult)
                    v.tensor_tensor(out=odb[jpar][0:n_part, 0:w_words],
                                    in0=mag16[0:n_part, 0:w16][:, 1::2],
                                    in1=sc32b[0:n_part, 0:w16][:, 1::2],
                                    op=OP.mult).then_inc(s_ve.h, 1)
                pl_ve = s_ve.inc(1)

                @block.sync
                def _(s):
                    s.wait_ge(s_ve.h, pl_ve)
                    base = gu8 if kind == "gu" else dn8
                    for h, buf in ((0, evb[jpar]), (1, odb[jpar])):
                        dst = base[128 * a:128 * a + n_part, 4 * h + jp, :]
                        s.dma_start(out=dst, in_=buf[0:n_part, 0:w_words]
                                    ).then_inc(s_q[jpar].h, 16)
                pl_store[(jpar, "st")] = s_q[jpar].inc(32)

            rses = {}
            jpars = {}
            for jp in range(4):
                jpars[jp] = n_jp_done % 2
                n_jp_done += 1
            rses[0] = emitA(0, jpars[0])
            rses[1] = emitA(1, jpars[1])
            emitB(0, jpars[0], rses[0])
            rses[2] = emitA(2, jpars[2])
            emitB(1, jpars[1], rses[1])
            rses[3] = emitA(3, jpars[3])
            emitB(2, jpars[2], rses[2])
            emitB(3, jpars[3], rses[3])
            tile_ve[ti] = s_ve.v

        gu_v = deq_gu_s.rearrange("(t x) (l o) -> x t l o", x=128, o=128)
        dn_v = deq_dn_s.rearrange("(t x) (l o) -> x t l o", x=128, o=128)

        @block.sync
        def _(s):
            s.wait_ge(s_q[0].h, s_q[0].v)
            s.wait_ge(s_q[1].h, s_q[1].v)
            for l in range(11):
                dst = deq_gu_loc[l * 128:(l + 1) * 128, :].rearrange(
                    "x (t o) -> x t o", o=128)
                s.dma_start(out=dst, in_=gu_v[:, :, l, :]).then_inc(s_sp.h, 16)
            for l in range(2):
                for th in range(2):
                    dst = deq_dn_loc[l * 128:(l + 1) * 128,
                                     2816 * th:2816 * (th + 1)].rearrange(
                        "x (t o) -> x t o", o=128)
                    s.dma_start(out=dst, in_=dn_v[:, 22 * th:22 * (th + 1), l, :]
                                ).then_inc(s_sp.h, 16)
        s_sp.inc(16 * 15)
        deq_sp_done = s_sp.v
        deq_ve_done = s_ve.v

        @block.gpsimd
        def _(gp):
            gp.wait_ge(s_sp.h, deq_sp_done)
            gp.collective_compute("AllGather", OP.bypass,
                                  replica_groups=[list(range(8))],
                                  ins=[deq_gu_loc], outs=[deq_gu_all]
                                  ).then_inc(s_ag[0].h, 1)
            gp.collective_compute("AllGather", OP.bypass,
                                  replica_groups=[list(range(8))],
                                  ins=[deq_dn_loc], outs=[deq_dn_all]
                                  ).then_inc(s_ag[1].h, 1)
        s_ag[0].inc(1)
        s_ag[1].inc(1)

        TM_OFF = {0: 2 * CT, 1: 0, 2: CT}
        WB_OFF = {0: 0, 1: 1, 2: 2}

        ffn_done_ve = 0
        sc_sp_prev = None
        for g in range(G):
            @block.vector
            def _(v, g=g):
                v.wait_ge(s_se.h, s_se.v)
                if g == 0:
                    v.wait_ge(s_ve.h, prevhs_ve)
                    v.wait_ge(s_ve.h, deq_ve_done)
                    v.tensor_copy(out=hs3[:, :, 0], in_=prevhs[:, :]).then_inc(s_ve.h, 1)
                else:
                    v.tensor_copy(out=hs3[:, :, 0], in_=hs3[:, :, 1024]).then_inc(s_ve.h, 1)
            s_ve.inc(1)

            tau_ld = [0] * NT
            tau_hs = [0] * NT
            tau_tr0 = [0] * NT
            tau_tr1 = [0] * NT
            tau_ev_v = [0] * NT
            tau_ev_s = [0] * NT
            for tau in range(NT):
                sl = tau % 2

                def _ld(e, g=g, tau=tau, sl=sl):
                    if g == 0 and tau < 2:
                        e.wait_ge(s_ve.h, deq_ve_done)
                    if tau >= 2:
                        e.wait_ge(s_ve.h, tau_hs[tau - 2])
                    r0 = g * TG + tau * 128
                    e.dma_start(out=xrow_a[sl][:, :], in_=x_in[r0:r0 + 128, :]
                                ).then_inc(s_x[sl].h, 16)
                if g == 0:
                    block.sync(_ld)
                else:
                    block.gpsimd(_ld)
                tau_ld[tau] = s_x[sl].inc(16)

                @block.vector
                def _(v, tau=tau, sl=sl):
                    v.wait_ge(s_x[sl].h, tau_ld[tau])
                    v.reduce_sum(out=stat_sum[:, tau:tau + 1], in_=xrow_a[sl][:, :],
                                 axis=AX).then_inc(s_ve.h, 1)
                s_ve.inc(1)

                @block.scalar
                def _(sc, tau=tau, sl=sl):
                    sc.wait_ge(s_x[sl].h, tau_ld[tau])
                    sc.activation(out=junk[:, :], in_=xrow_a[sl][:, :], func=AF.Square,
                                  accum_out=stat_sq[:, tau:tau + 1]).then_inc(s_se.h, 1)
                s_se.inc(1)
                sq_se = s_se.v

                @block.vector
                def _(v, tau=tau):
                    v.wait_ge(s_se.h, sq_se)
                    v.tensor_scalar(out=stat_t[:, tau:tau + 1], in0=stat_sq[:, tau:tau + 1],
                                    scalar1=INV_D, scalar2=None, op0=OP.mult)
                    v.tensor_scalar(out=stat_m[:, tau:tau + 1], in0=stat_sum[:, tau:tau + 1],
                                    scalar1=INV_D, scalar2=None, op0=OP.mult)
                    v.tensor_copy(out=gapc[:, :], in_=ones_c[:, :])
                    v.scalar_tensor_tensor(out=stat_r[:, tau:tau + 1],
                                           in0=stat_m[:, tau:tau + 1],
                                           scalar=stat_m[:, tau:tau + 1],
                                           in1=stat_t[:, tau:tau + 1],
                                           op0=OP.mult, op1=OP.subtract
                                           ).then_inc(s_ve.h, 1)
                s_ve.inc(1)
                var_ve = s_ve.v

                @block.scalar
                def _(sc, tau=tau):
                    sc.wait_ge(s_ve.h, var_ve)
                    sc.activation(out=stat_r[:, tau:tau + 1], in_=stat_r[:, tau:tau + 1],
                                  func=AF.Ln, scale=-1.0, bias=eps1_c[:, :])
                    sc.activation(out=stat_r[:, tau:tau + 1], in_=stat_r[:, tau:tau + 1],
                                  func=AF.Exp, scale=-0.5).then_inc(s_se.h, 1)
                s_se.inc(1)
                rstd_se = s_se.v

                @block.vector
                def _(v, tau=tau, sl=sl):
                    v.wait_ge(s_se.h, rstd_se)
                    if tau >= 1:
                        v.wait_ge(s_te.h, tau_tr1[tau - 1])
                    v.tensor_scalar(out=hsrow[:, :], in0=xrow_a[sl][:, :],
                                    scalar1=stat_m[:, tau:tau + 1],
                                    scalar2=stat_r[:, tau:tau + 1],
                                    op0=OP.subtract, op1=OP.mult).then_inc(s_ve.h, 1)
                tau_hs[tau] = s_ve.inc(1)

                @block.tensor
                def _(t, tau=tau):
                    t.wait_ge(s_ve.h, tau_hs[tau])
                    if tau >= 1:
                        t.wait_ge(s_ve.h, tau_ev_v[tau - 1])
                        t.wait_ge(s_se.h, tau_ev_s[tau - 1])
                    for i in range(8):
                        ins = t.transpose(psT[0][:, i * 128:(i + 1) * 128],
                                          hsrow[:, i * 128:(i + 1) * 128], idb_sb[:, :])
                    ins.then_inc(s_te.h, 1)
                    s_te.inc(1)
                    for i in range(8):
                        ins = t.transpose(psT[1][:, i * 128:(i + 1) * 128],
                                          hsrow[:, (8 + i) * 128:(9 + i) * 128],
                                          idb_sb[:, :])
                    ins.then_inc(s_te.h, 1)
                    s_te.inc(1)
                tau_tr1[tau] = s_te.v
                tau_tr0[tau] = s_te.v - 1

                @block.vector
                def _(v, g=g, tau=tau):
                    v.wait_ge(s_te.h, tau_tr0[tau])
                    src = psT[0][:, 0:1024].rearrange("p (c t) -> p c t", t=128)
                    v.tensor_copy(out=hs3[:, 0:8, 1 + tau * 128:1 + (tau + 1) * 128],
                                  in_=src).then_inc(s_ve.h, 1)
                tau_ev_v[tau] = s_ve.inc(1)

                @block.scalar
                def _(sc, g=g, tau=tau):
                    sc.wait_ge(s_te.h, tau_tr1[tau])
                    src = psT[1][:, 0:1024].rearrange("p (c t) -> p c t", t=128)
                    sc.copy(out=hs3[:, 8:16, 1 + tau * 128:1 + (tau + 1) * 128],
                            in_=src).then_inc(s_se.h, 1)
                tau_ev_s[tau] = s_se.inc(1)

            hs_done_ve = s_ve.v
            hs_done_se = s_se.v

            mix_done = {}

            def emit_mix(pj, slot, extra_te_wait=None):
                @block.vector
                def _(v, pj=pj, slot=slot):
                    v.wait_ge(s_ve.h, hs_done_ve)
                    v.wait_ge(s_se.h, hs_done_se)
                    if extra_te_wait is not None:
                        v.wait_ge(s_te.h, extra_te_wait)
                    tmo = TM_OFF[pj]
                    for ct in range(CT):
                        hs_c = cmcol(ct, 1, 1025)
                        xx_c = cmcol(ct, 0, 1024)
                        v.tensor_tensor(out=dmix[:, :], in0=hs_c, in1=xx_c, op=OP.subtract)
                        ins = v.scalar_tensor_tensor(
                            out=xM3[slot][:, ct, :], in0=dmix[:, :],
                            scalar=tmix_sb[:, tmo + ct:tmo + ct + 1],
                            in1=xx_c, op0=OP.mult, op1=OP.add)
                    ins.then_inc(s_ve.h, 1)
                mix_done[pj] = s_ve.inc(1)

            emit_mix(0, 0)
            if DEBUG_TAPS and DEBUG_TAPS_MID and g == 0:
                @block.sync
                def _(s):
                    s.wait_ge(s_ve.h, mix_done[0])
                    s.dma_start(out=dbg_hs, in_=A_cm[:, 0:16400]).then_inc(s_sp.h, 16)
                    s.dma_start(out=dbg_xr, in_=A_xM[:, 0:16384]).then_inc(s_sp.h, 16)
                    s.dma_start(out=dbg_st, in_=smallF[:, :]).then_inc(s_sp.h, 16)
                    s.wait_ge(s_sp.h, s_sp.v + 48)
                s_sp.inc(48)
            emit_mix(1, 1)

            r_pan_ld = [0] * CT
            r_te = [0] * CT
            r_se = [0] * CT
            r_sp = [0] * CT
            for ot in range(CT):
                @block.sync
                def _(s, g=g, ot=ot):
                    if ot < 2:
                        s.wait_ge(s_ve.h, mix_done[0])
                        s.wait_ge(s_te.h, s_te.v)
                        if g == 0:
                            s.wait_ge(s_q[0].h, cast_sp[6])
                            s.wait_ge(s_q[1].h, cast_sp[7])
                    else:
                        s.wait_ge(s_te.h, r_te[ot - 2])
                    src = wbfT[(0 * CT + ot) * 128:(0 * CT + ot) * 128 + 128, :]
                    s.dma_start(out=panW[ot % 2][:, :, :],
                                in_=src.rearrange("p (k o) -> p k o", o=128)
                                ).then_inc(s_p[ot % 2].h, 16)
                r_pan_ld[ot] = s_p[ot % 2].inc(16)

                @block.tensor
                def _(t, ot=ot):
                    t.wait_ge(s_p[ot % 2].h, r_pan_ld[ot])
                    t.wait_ge(s_ve.h, mix_done[0])
                    if ot >= 3:
                        t.wait_ge(s_se.h, r_se[ot - 3])
                    ps = psF[ot % 3]
                    for kt in range(CT):
                        t.matmul(ps[:, 0:512], panW[ot % 2][:, kt, :],
                                 xM3[0][:, kt, 0:512], start=(kt == 0), stop=(kt == CT - 1))
                        ins = t.matmul(ps[:, 512:1024], panW[ot % 2][:, kt, :],
                                       xM3[0][:, kt, 512:1024],
                                       start=(kt == 0), stop=(kt == CT - 1))
                    ins.then_inc(s_te.h, 1)
                r_te[ot] = s_te.inc(1)

                @block.scalar
                def _(sc, ot=ot):
                    sc.wait_ge(s_te.h, r_te[ot])
                    if ot >= 2:
                        sc.wait_ge(s_q[ot % 2].h, r_sp[ot - 2])
                    sc.activation(out=mrowA[ot % 2][:, :], in_=psF[ot % 3][:, :],
                                  func=AF.Sigmoid).then_inc(s_se.h, 1)
                r_se[ot] = s_se.inc(1)

                @block.gpsimd
                def _(gp, g=g, ot=ot):
                    gp.wait_ge(s_se.h, r_se[ot])
                    gp.dma_start(out=sp_M[128 * ot:128 * (ot + 1), g * TG:(g + 1) * TG],
                                 in_=mrowA[ot % 2][:, :]).then_inc(s_q[ot % 2].h, 16)
                r_sp[ot] = s_q[ot % 2].inc(16)

            emit_mix(2, 0, extra_te_wait=r_te[CT - 1])

            kv_pan_ld = [0] * CT
            te_k = [0] * CT
            te_v = [0] * CT
            se_k = [0] * CT
            ve_kv = [0] * CT
            for ot in range(CT):
                @block.sync
                def _(s, g=g, ot=ot):
                    if ot < 2:
                        s.wait_ge(s_ve.h, mix_done[1])
                        s.wait_ge(s_te.h, r_te[CT - 1])
                        if g == 0:
                            s.wait_ge(s_q[0].h, cast_sp[22])
                            s.wait_ge(s_q[1].h, cast_sp[23])
                    else:
                        s.wait_ge(s_te.h, te_v[ot - 2])
                    for woff, dst3 in ((1, panW[ot % 2]), (2, panU[ot % 2])):
                        src = wbfT[(woff * CT + ot) * 128:(woff * CT + ot) * 128 + 128, :]
                        s.dma_start(out=dst3[:, :, :],
                                    in_=src.rearrange("p (k o) -> p k o", o=128)
                                    ).then_inc(s_p[ot % 2].h, 16)
                kv_pan_ld[ot] = s_p[ot % 2].inc(32)

                @block.tensor
                def _(t, ot=ot):
                    t.wait_ge(s_p[ot % 2].h, kv_pan_ld[ot])
                    t.wait_ge(s_ve.h, mix_done[1])
                    if ot == 0:
                        t.wait_ge(s_se.h, r_se[CT - 1])
                    if ot >= 2:
                        t.wait_ge(s_ve.h, ve_kv[ot - 2])
                    psk = psF[(2 * ot) % 3]
                    for kt in range(CT):
                        t.matmul(psk[:, 0:512], panW[ot % 2][:, kt, :],
                                 xM3[1][:, kt, 0:512], start=(kt == 0), stop=(kt == CT - 1))
                        ins = t.matmul(psk[:, 512:1024], panW[ot % 2][:, kt, :],
                                       xM3[1][:, kt, 512:1024],
                                       start=(kt == 0), stop=(kt == CT - 1))
                    ins.then_inc(s_te.h, 1)
                te_k[ot] = s_te.inc(1)

                @block.scalar
                def _(sc, ot=ot):
                    sc.wait_ge(s_te.h, te_k[ot])
                    if ot >= 1:
                        sc.wait_ge(s_ve.h, ve_kv[ot - 1])
                    sc.copy(out=ktmp[:, :], in_=psF[(2 * ot) % 3][:, :]).then_inc(s_se.h, 1)
                se_k[ot] = s_se.inc(1)

                @block.tensor
                def _(t, ot=ot):
                    t.wait_ge(s_ve.h, mix_done[2])
                    if ot >= 1:
                        t.wait_ge(s_se.h, se_k[ot - 1])
                    psv = psF[(2 * ot + 1) % 3]
                    for kt in range(CT):
                        t.matmul(psv[:, 0:512], panU[ot % 2][:, kt, :],
                                 xM3[0][:, kt, 0:512], start=(kt == 0), stop=(kt == CT - 1))
                        ins = t.matmul(psv[:, 512:1024], panU[ot % 2][:, kt, :],
                                       xM3[0][:, kt, 512:1024],
                                       start=(kt == 0), stop=(kt == CT - 1))
                    ins.then_inc(s_te.h, 1)
                te_v[ot] = s_te.inc(1)

                @block.vector
                def _(v, ot=ot):
                    v.wait_ge(s_te.h, te_v[ot])
                    v.wait_ge(s_se.h, se_k[ot])
                    v.tensor_tensor(out=kv3[:, ot, :], in0=psF[(2 * ot + 1) % 3][:, :],
                                    in1=ktmp[:, :], op=OP.mult).then_inc(s_ve.h, 1)
                ve_kv[ot] = s_ve.inc(1)

            if DEBUG_TAPS and DEBUG_TAPS_MID and g == 0:
                @block.sync
                def _(s):
                    s.wait_ge(s_ve.h, ve_kv[CT - 1])
                    s.dma_start(out=dbg_kv, in_=A_big[:, 0:16384]).then_inc(s_sp.h, 16)
                    s.wait_ge(s_sp.h, s_sp.v + 16)
                s_sp.inc(16)

            sc_sp = [0] * CT
            for ct in range(CT):
                @block.vector
                def _(v, g=g, ct=ct, sc_sp_prev=sc_sp_prev):
                    v.wait_ge(s_ve.h, ve_kv[ct])
                    if ct >= 2:
                        v.wait_ge(s_q[ct % 2].h, sc_sp[ct - 2])
                    elif sc_sp_prev is not None:
                        v.wait_ge(s_q[ct % 2].h, sc_sp_prev[CT - 2 + ct])
                    v.tensor_copy(out=kvf32[:, :], in_=kv3[:, ct, :])
                    v.tensor_scalar(out=wt_f[:, :], in0=zrow[:, :],
                                    scalar1=wdec_sb[:, ct:ct + 1], scalar2=None,
                                    op0=OP.add)
                    v.tensor_copy(out=S_buf[:, 0:1], in_=sbound[:, ct:ct + 1])
                    v.tensor_tensor_scan(out=S_buf[:, 1:1025], data0=wt_f[:, :],
                                         data1=kvf32[:, :],
                                         initial=sbound[:, ct:ct + 1],
                                         op0=OP.mult, op1=OP.add)
                    v.scalar_tensor_tensor(out=prowA[ct % 2][:, :], in0=kvf32[:, :],
                                           scalar=tfv_sb[:, ct:ct + 1],
                                           in1=S_buf[:, 0:1024],
                                           op0=OP.mult, op1=OP.add)
                    v.tensor_copy(out=sbound[:, ct:ct + 1],
                                  in_=S_buf[:, 1024:1025]).then_inc(s_ve.h, 1)
                p_ve = s_ve.inc(1)

                @block.gpsimd
                def _(gp, g=g, ct=ct, p_ve=p_ve):
                    gp.wait_ge(s_ve.h, p_ve)
                    gp.dma_start(out=sp_P[128 * ct:128 * (ct + 1), g * TG:(g + 1) * TG],
                                 in_=prowA[ct % 2][:, :]).then_inc(s_q[ct % 2].h, 16)
                sc_sp[ct] = s_q[ct % 2].inc(16)

            sc_sp_prev = sc_sp

        a_done_ve = s_ve.v

        @block.sync
        def _(s):
            s.wait_ge(s_ve.h, a_done_ve)
            s.dma_start(out=f_loc, in_=sbound[:, :]).then_inc(s_sp.h, 16)
        f_sp = s_sp.inc(16)

        @block.gpsimd
        def _(gp):
            gp.wait_ge(s_sp.h, f_sp)
            gp.collective_compute("AllGather", OP.bypass,
                                  replica_groups=[list(range(8))],
                                  ins=[f_loc], outs=[f_all]).then_inc(s_ag[2].h, 1)
            gp.wait_ge(s_ag[2].h, 1)
            for r in range(8):
                gp.dma_start(out=fall_sb[:, 16 * r:16 * (r + 1)],
                             in_=f_all[128 * r:128 * (r + 1), :]).then_inc(s_gld.h, 16)
        s_ag[2].inc(1)
        fall_ld = s_gld.inc(16 * 8)

        for ct in range(CT):
            @block.vector
            def _(v, ct=ct):
                if ct == 0:
                    v.wait_ge(s_gld.h, fall_ld)
                v.tensor_scalar(out=c8a[:, :], in0=steps_b[:, :],
                                scalar1=nexp_sb[:, ct:ct + 1], scalar2=None,
                                op0=OP.mult).then_inc(s_ve.h, 1)
            ca_ve = s_ve.inc(1)

            @block.scalar
            def _(sc, ct=ct, ca_ve=ca_ve):
                sc.wait_ge(s_ve.h, ca_ve)
                sc.activation(out=c8b[:, :], in_=c8a[:, :], func=AF.Exp).then_inc(s_se.h, 1)
            cb_se = s_se.inc(1)

            @block.vector
            def _(v, ct=ct, cb_se=cb_se):
                v.wait_ge(s_se.h, cb_se)
                v.tensor_tensor(out=c8a[:, :], in0=c8b[:, :], in1=mask_b[:, :], op=OP.mult)
                v.tensor_copy(out=gapc[:, :], in_=ones_c[:, :])
                fsl = fall_sb[:, :].rearrange("p (r c) -> p r c", c=CT)[:, :, ct]
                v.tensor_tensor(out=c8b[:, :], in0=c8a[:, :], in1=fsl, op=OP.mult)
                v.tensor_copy(out=gapc[:, :], in_=ones_c[:, :])
                v.reduce_sum(out=sin_sb[:, ct:ct + 1], in_=c8b[:, :], axis=AX
                             ).then_inc(s_ve.h, 1)
            s_ve.inc(1)
        sin_ve = s_ve.v

        for g in range(G):
            b_ld = [0] * CT
            wkv_ve = [0] * CT
            for ct in range(CT):
                @block.sync
                def _(s, g=g, ct=ct):
                    if ct < 2:
                        s.wait_ge(s_ve.h, sin_ve if g == 0 else ffn_done_ve)
                        s.wait_ge(s_sp.h, s_sp.v)
                        s.wait_ge(s_q[0].h, s_q[0].v)
                        s.wait_ge(s_q[1].h, s_q[1].v)
                        s.wait_ge(s_out.h, s_out.v)
                        s.wait_ge(s_te.h, s_te.v)
                    else:
                        s.wait_ge(s_ve.h, wkv_ve[ct - 2])
                    s.dma_start(out=prowB[ct % 2][:, :],
                                in_=sp_P[128 * ct:128 * (ct + 1), g * TG:(g + 1) * TG]
                                ).then_inc(s_b[ct % 2].h, 16)
                    s.dma_start(out=mrowB[ct % 2][:, :],
                                in_=sp_M[128 * ct:128 * (ct + 1), g * TG:(g + 1) * TG]
                                ).then_inc(s_b[ct % 2].h, 16)
                b_ld[ct] = s_b[ct % 2].inc(32)

                @block.vector
                def _(v, g=g, ct=ct):
                    v.wait_ge(s_b[ct % 2].h, b_ld[ct])
                    v.tensor_scalar(out=wt_f[:, :], in0=zrow[:, :],
                                    scalar1=wdec_sb[:, ct:ct + 1], scalar2=None, op0=OP.add)
                    v.tensor_copy(out=wp_buf[:, 0:1],
                                  in_=pball[:, g * CT + ct:g * CT + ct + 1])
                    v.tensor_tensor_scan(out=wp_buf[:, 1:1025], data0=wt_f[:, :],
                                         data1=zrow[:, :],
                                         initial=pball[:, g * CT + ct:g * CT + ct + 1],
                                         op0=OP.mult, op1=OP.add)
                    v.scalar_tensor_tensor(out=kvf32[:, :], in0=wp_buf[:, 0:1024],
                                           scalar=sin_sb[:, ct:ct + 1],
                                           in1=prowB[ct % 2][:, :], op0=OP.mult, op1=OP.add)
                    v.tensor_tensor(out=wkv3[:, ct, :], in0=kvf32[:, :],
                                    in1=mrowB[ct % 2][:, :], op=OP.mult).then_inc(s_ve.h, 1)
                wkv_ve[ct] = s_ve.inc(1)
            wkv_all = s_ve.v

            tt_mm_te = [0] * (2 * NT)
            tt_ev_se = [0] * (2 * NT)
            x2ln_ve = [0] * NT
            b3sp = [0] * NT
            trB_ve = [0] * NT
            trB_se = [0] * NT
            trB_te_last = [0] * NT
            wo_ld = [0, 0]
            for half in range(2):
                @block.sync
                def _(s, g=g, half=half):
                    if half == 0:
                        s.wait_ge(s_ve.h, wkv_all)
                        if g == 0:
                            s.wait_ge(s_q[0].h, cast_sp[30])
                            s.wait_ge(s_q[1].h, cast_sp[31])
                    else:
                        s.wait_ge(s_te.h, tt_mm_te[NT - 1])
                    for kt in range(CT):
                        s.dma_start(out=woT3[:, kt, :],
                                    in_=wbf[128 * kt:128 * (kt + 1),
                                            1024 * half:1024 * (half + 1)]
                                    ).then_inc(s_ld.h, 16)
                wo_ld[half] = s_ld.inc(16 * CT)

                for tt in range(NT):
                    idx = NT * half + tt

                    @block.tensor
                    def _(t, tt=tt, half=half, idx=idx):
                        if tt == 0:
                            t.wait_ge(s_ld.h, wo_ld[half])
                        if idx >= 3:
                            t.wait_ge(s_se.h, tt_ev_se[idx - 3])
                        ps = psF[idx % 3]
                        for ot in range(CT):
                            lhsT = wkv3[:, ot, 128 * tt:128 * (tt + 1)]
                            t.matmul(ps[:, 0:512], lhsT, woT3[:, ot, 0:512],
                                     start=(ot == 0), stop=(ot == CT - 1))
                            ins = t.matmul(ps[:, 512:1024], lhsT, woT3[:, ot, 512:1024],
                                           start=(ot == 0), stop=(ot == CT - 1))
                        ins.then_inc(s_te.h, 1)
                    tt_mm_te[idx] = s_te.inc(1)

                    @block.scalar
                    def _(sc, tt=tt, half=half, idx=idx):
                        sc.wait_ge(s_te.h, tt_mm_te[idx])
                        sc.copy(out=orow_full[:, 2048 * tt + 1024 * half:
                                              2048 * tt + 1024 * (half + 1)],
                                in_=psF[idx % 3][:, :]).then_inc(s_se.h, 1)
                    tt_ev_se[idx] = s_se.inc(1)

            for tt in range(NT):
                orow = orow_full[:, 2048 * tt:2048 * (tt + 1)]

                @block.sync
                def _(s, g=g, tt=tt):
                    if tt >= 1:
                        s.wait_ge(s_ve.h, x2ln_ve[tt - 1])
                    r0 = g * TG + tt * 128
                    s.dma_start(out=xresB[:, :], in_=x_in[r0:r0 + 128, :]
                                ).then_inc(s_x[0].h, 16)
                xres_ld = s_x[0].inc(16)

                @block.vector
                def _(v, tt=tt, orow=orow):
                    v.wait_ge(s_se.h, tt_ev_se[NT + tt])
                    v.reduce_sum(out=stat_sum[:, 0:1], in_=orow, axis=AX
                                 ).then_inc(s_ve.h, 1)
                os_ve = s_ve.inc(1)

                @block.scalar
                def _(sc, tt=tt, os_ve=os_ve, orow=orow):
                    sc.wait_ge(s_ve.h, os_ve)
                    sc.activation(out=junk[:, :], in_=orow, func=AF.Square,
                                  accum_out=stat_sq[:, 0:1]).then_inc(s_se.h, 1)
                osq_se = s_se.inc(1)

                @block.vector
                def _(v, tt=tt, osq_se=osq_se):
                    v.wait_ge(s_se.h, osq_se)
                    v.tensor_scalar(out=stat_t[:, 0:1], in0=stat_sq[:, 0:1],
                                    scalar1=INV_D, scalar2=None, op0=OP.mult)
                    v.tensor_scalar(out=stat_m[:, 0:1], in0=stat_sum[:, 0:1],
                                    scalar1=INV_D, scalar2=None, op0=OP.mult)
                    v.tensor_copy(out=gapc[:, :], in_=ones_c[:, :])
                    v.scalar_tensor_tensor(out=stat_r[:, 0:1], in0=stat_m[:, 0:1],
                                           scalar=stat_m[:, 0:1], in1=stat_t[:, 0:1],
                                           op0=OP.mult, op1=OP.subtract)
                    v.tensor_scalar(out=stat_u[:, 0:1], in0=stat_m[:, 0:1],
                                    scalar1=-1.0, scalar2=None,
                                    op0=OP.mult).then_inc(s_ve.h, 1)
                ov_ve = s_ve.inc(1)

                @block.scalar
                def _(sc, tt=tt, ov_ve=ov_ve):
                    sc.wait_ge(s_ve.h, ov_ve)
                    sc.activation(out=stat_r[:, 0:1], in_=stat_r[:, 0:1], func=AF.Ln,
                                  scale=-1.0, bias=eps1_c[:, :])
                    sc.activation(out=stat_r[:, 0:1], in_=stat_r[:, 0:1], func=AF.Exp,
                                  scale=-0.5).then_inc(s_se.h, 1)
                or_se = s_se.inc(1)

                @block.vector
                def _(v, tt=tt, or_se=or_se, xres_ld=xres_ld, orow=orow):
                    v.wait_ge(s_se.h, or_se)
                    v.wait_ge(s_x[0].h, xres_ld)
                    if tt >= 1:
                        v.wait_ge(s_sp.h, b3sp[tt - 1])
                    v.tensor_tensor(out=stat_v[:, 0:1], in0=stat_u[:, 0:1],
                                    in1=stat_r[:, 0:1], op=OP.mult)
                    v.tensor_copy(out=gapc[:, :], in_=ones_c[:, :])
                    v.tensor_scalar(out=x2row[:, :], in0=orow,
                                    scalar1=stat_r[:, 0:1], scalar2=stat_v[:, 0:1],
                                    op0=OP.mult, op1=OP.add)
                    v.tensor_tensor(out=x2row[:, :], in0=x2row[:, :],
                                    in1=xresB[:, :], op=OP.add)
                    v.reduce_sum(out=stat_sum[:, 0:1], in_=x2row[:, :], axis=AX
                                 ).then_inc(s_ve.h, 1)
                x2_ve = s_ve.inc(1)

                @block.scalar
                def _(sc, tt=tt, x2_ve=x2_ve):
                    sc.wait_ge(s_ve.h, x2_ve)
                    sc.activation(out=junk[:, :], in_=x2row[:, :], func=AF.Square,
                                  accum_out=stat_sq[:, 0:1]).then_inc(s_se.h, 1)
                x2q_se = s_se.inc(1)

                @block.vector
                def _(v, tt=tt, x2q_se=x2q_se):
                    v.wait_ge(s_se.h, x2q_se)
                    v.tensor_scalar(out=stat_t[:, 0:1], in0=stat_sq[:, 0:1],
                                    scalar1=INV_D, scalar2=None, op0=OP.mult)
                    v.tensor_scalar(out=stat_m[:, 0:1], in0=stat_sum[:, 0:1],
                                    scalar1=INV_D, scalar2=None, op0=OP.mult)
                    v.tensor_copy(out=gapc[:, :], in_=ones_c[:, :])
                    v.scalar_tensor_tensor(out=stat_r[:, 0:1], in0=stat_m[:, 0:1],
                                           scalar=stat_m[:, 0:1], in1=stat_t[:, 0:1],
                                           op0=OP.mult, op1=OP.subtract).then_inc(s_ve.h, 1)
                x2v_ve = s_ve.inc(1)

                @block.scalar
                def _(sc, tt=tt, x2v_ve=x2v_ve):
                    sc.wait_ge(s_ve.h, x2v_ve)
                    sc.activation(out=stat_r[:, 0:1], in_=stat_r[:, 0:1], func=AF.Ln,
                                  scale=-1.0, bias=eps1_c[:, :])
                    sc.activation(out=stat_r[:, 0:1], in_=stat_r[:, 0:1], func=AF.Exp,
                                  scale=-0.5).then_inc(s_se.h, 1)
                x2r_se = s_se.inc(1)

                @block.vector
                def _(v, tt=tt, x2r_se=x2r_se):
                    v.wait_ge(s_se.h, x2r_se)
                    if tt >= 1:
                        v.wait_ge(s_te.h, trB_te_last[tt - 1])
                    v.tensor_scalar(out=x2lnrow[:, :], in0=x2row[:, :],
                                    scalar1=stat_m[:, 0:1], scalar2=stat_r[:, 0:1],
                                    op0=OP.subtract, op1=OP.mult).then_inc(s_ve.h, 1)
                x2ln_ve[tt] = s_ve.inc(1)

                @block.sync
                def _(s, g=g, tt=tt):
                    s.wait_ge(s_ve.h, x2ln_ve[tt])
                    r0 = g * TG + tt * 128
                    s.dma_start(out=sp_x2[r0:r0 + 128, :], in_=x2row[:, :]
                                ).then_inc(s_sp.h, 16)
                b3sp[tt] = s_sp.inc(16)

                @block.tensor
                def _(t, tt=tt):
                    t.wait_ge(s_ve.h, x2ln_ve[tt])
                    if tt >= 1:
                        t.wait_ge(s_ve.h, trB_ve[tt - 1])
                        t.wait_ge(s_se.h, trB_se[tt - 1])
                    for i in range(8):
                        ins = t.transpose(psT[0][:, i * 128:(i + 1) * 128],
                                          x2lnrow[:, i * 128:(i + 1) * 128], idb_sb[:, :])
                    ins.then_inc(s_te.h, 1)
                    s_te.inc(1)
                    for i in range(8):
                        ins = t.transpose(psT[1][:, i * 128:(i + 1) * 128],
                                          x2lnrow[:, (8 + i) * 128:(9 + i) * 128],
                                          idb_sb[:, :])
                    ins.then_inc(s_te.h, 1)
                    s_te.inc(1)
                trB_te_last[tt] = s_te.v

                @block.vector
                def _(v, tt=tt):
                    v.wait_ge(s_te.h, trB_te_last[tt] - 1)
                    src = psT[0][:, 0:1024].rearrange("p (c t) -> p c t", t=128)
                    v.tensor_copy(out=x2ln3[:, 0:8, 128 * tt:128 * (tt + 1)],
                                  in_=src).then_inc(s_ve.h, 1)
                trB_ve[tt] = s_ve.inc(1)

                @block.scalar
                def _(sc, tt=tt):
                    sc.wait_ge(s_te.h, trB_te_last[tt])
                    src = psT[1][:, 0:1024].rearrange("p (c t) -> p c t", t=128)
                    sc.copy(out=x2ln3[:, 8:16, 128 * tt:128 * (tt + 1)],
                            in_=src).then_inc(s_se.h, 1)
                trB_se[tt] = s_se.inc(1)

            x2ln_done_ve = s_ve.v
            x2ln_done_se = s_se.v
            b3_sp_done = s_sp.v

            gu_ld = [0] * 44
            gu_te = [0] * 44
            sg_se = [0] * 44
            h_ve = [0] * 44
            dn_te = {}
            y2_ev = {}
            for half in range(2):
                for il in range(22):
                    it = half * 22 + il

                    @block.sync
                    def _(s, g=g, it=it, half=half, il=il):
                        if it < 2:
                            s.wait_ge(s_ve.h, x2ln_done_ve)
                            s.wait_ge(s_se.h, x2ln_done_se)
                            s.wait_ge(s_te.h, s_te.v)
                            if g == 0 and half == 0:
                                s.wait_ge(s_ag[0].h, 1)
                                s.wait_ge(s_ag[1].h, 1)
                        elif half == 1 and il < 2:
                            s.wait_ge(s_te.h, dn_te[(0, CT - 1)])
                            s.wait_ge(s_te.h, gu_te[it - 2])
                        else:
                            s.wait_ge(s_te.h, gu_te[it - 2])
                        for base_o, dst3 in ((128 * it, panW[it % 2]),
                                             (I_FF + 128 * it, panU[it % 2])):
                            blk, c0 = divmod(base_o, GU_OSL)
                            r0 = GU_OSL * blk + c0
                            src = deq_gu_all[r0:r0 + 128, :]
                            s.dma_start(out=dst3[:, :, :],
                                        in_=src.rearrange("p (k o) -> p k o", o=128)
                                        ).then_inc(s_p[it % 2].h, 16)
                    gu_ld[it] = s_p[it % 2].inc(32)

                    @block.tensor
                    def _(t, it=it, half=half, il=il):
                        t.wait_ge(s_p[it % 2].h, gu_ld[it])
                        if it >= 1:
                            t.wait_ge(s_se.h, sg_se[it - 1])
                        if it >= 2:
                            t.wait_ge(s_ve.h, h_ve[it - 2])
                        psg = psF[(2 * it) % 3]
                        psu = psF[(2 * it + 1) % 3]
                        for kt in range(CT):
                            t.matmul(psg[:, 0:512], panW[it % 2][:, kt, :],
                                     x2ln3[:, kt, 0:512], start=(kt == 0), stop=(kt == CT - 1))
                            ins = t.matmul(psg[:, 512:1024], panW[it % 2][:, kt, :],
                                           x2ln3[:, kt, 512:1024],
                                           start=(kt == 0), stop=(kt == CT - 1))
                        ins.then_inc(s_te.h, 1)
                        s_te.inc(1)
                        for kt in range(CT):
                            t.matmul(psu[:, 0:512], panU[it % 2][:, kt, :],
                                     x2ln3[:, kt, 0:512], start=(kt == 0), stop=(kt == CT - 1))
                            ins = t.matmul(psu[:, 512:1024], panU[it % 2][:, kt, :],
                                           x2ln3[:, kt, 512:1024],
                                           start=(kt == 0), stop=(kt == CT - 1))
                        ins.then_inc(s_te.h, 1)
                        s_te.inc(1)
                    gu_te[it] = s_te.v

                    @block.scalar
                    def _(sc, it=it):
                        sc.wait_ge(s_te.h, gu_te[it] - 1)
                        if it >= 1:
                            sc.wait_ge(s_ve.h, h_ve[it - 1])
                        sc.activation(out=sgrow[:, :], in_=psF[(2 * it) % 3][:, :],
                                      func=AF.Silu).then_inc(s_se.h, 1)
                    sg_se[it] = s_se.inc(1)

                    @block.vector
                    def _(v, it=it, il=il):
                        v.wait_ge(s_te.h, gu_te[it])
                        v.wait_ge(s_se.h, sg_se[it])
                        if il == 0:
                            v.wait_ge(s_sp.h, b3_sp_done)
                        v.tensor_tensor(out=h3[:, il, :], in0=psF[(2 * it + 1) % 3][:, :],
                                        in1=sgrow[:, :], op=OP.mult).then_inc(s_ve.h, 1)
                    h_ve[it] = s_ve.inc(1)
                h_half_ve = s_ve.v

                dn_ld = [0] * CT
                for ht in range(CT):
                    @block.sync
                    def _(s, ht=ht, half=half):
                        if ht < 2:
                            s.wait_ge(s_ve.h, h_half_ve)
                        else:
                            s.wait_ge(s_te.h, dn_te[(half, ht - 2)])
                        blk, htl = divmod(ht, 2)
                        r0 = DN_HSL * blk + 128 * htl
                        src = deq_dn_all[r0:r0 + 128, 2816 * half:2816 * (half + 1)]
                        s.dma_start(out=panD[ht % 2][:, :, :],
                                    in_=src.rearrange("p (k o) -> p k o", o=128)
                                    ).then_inc(s_p[ht % 2].h, 16)
                    dn_ld[ht] = s_p[ht % 2].inc(16)

                    @block.tensor
                    def _(t, ht=ht, half=half):
                        t.wait_ge(s_p[ht % 2].h, dn_ld[ht])
                        key = (half, ht - 3)
                        if key in y2_ev:
                            t.wait_ge(y2_ev[key][0], y2_ev[key][1])
                        elif half == 1 and (0, CT - 1) in y2_ev and ht < 3:
                            t.wait_ge(y2_ev[(0, CT - 1)][0], y2_ev[(0, CT - 1)][1])
                        ps = psF[ht % 3]
                        for il in range(22):
                            t.matmul(ps[:, 0:512], panD[ht % 2][:, il, :],
                                     h3[:, il, 0:512], start=(il == 0), stop=(il == 21))
                            ins = t.matmul(ps[:, 512:1024], panD[ht % 2][:, il, :],
                                           h3[:, il, 512:1024],
                                           start=(il == 0), stop=(il == 21))
                        ins.then_inc(s_te.h, 1)
                    dn_te[(half, ht)] = s_te.inc(1)

                    if half == 0:
                        @block.scalar
                        def _(sc, ht=ht):
                            sc.wait_ge(s_te.h, dn_te[(0, ht)])
                            sc.copy(out=y23[:, ht, :], in_=psF[ht % 3][:, :]
                                    ).then_inc(s_se.h, 1)
                        y2_ev[(0, ht)] = (s_se.h, s_se.inc(1))
                    else:
                        @block.vector
                        def _(v, ht=ht):
                            v.wait_ge(s_te.h, dn_te[(1, ht)])
                            v.wait_ge(s_se.h, y2_ev[(0, ht)][1])
                            v.tensor_tensor(out=y23[:, ht, :], in0=psF[ht % 3][:, :],
                                            in1=y23[:, ht, :], op=OP.add).then_inc(s_ve.h, 1)
                        y2_ev[(1, ht)] = (s_ve.h, s_ve.inc(1))

            y2_last_ve = y2_ev[(1, CT - 1)][1]

            fin_ev_ve = [0] * NT
            fin_ve = [0] * NT
            fin_tr = [0] * NT
            outsp = [0] * NT
            for tt in range(NT):
                @block.tensor
                def _(t, tt=tt):
                    if tt == 0:
                        t.wait_ge(s_ve.h, y2_last_ve)
                    else:
                        t.wait_ge(s_ve.h, fin_ev_ve[tt - 1])
                    for i in range(8):
                        ins = t.transpose(psT[0][:, i * 128:(i + 1) * 128],
                                          y23[:, i, 128 * tt:128 * (tt + 1)], idb_sb[:, :])
                    ins.then_inc(s_te.h, 1)
                    s_te.inc(1)
                    for i in range(8):
                        ins = t.transpose(psT[1][:, i * 128:(i + 1) * 128],
                                          y23[:, 8 + i, 128 * tt:128 * (tt + 1)], idb_sb[:, :])
                    ins.then_inc(s_te.h, 1)
                    s_te.inc(1)
                fin_tr[tt] = s_te.v

                @block.vector
                def _(v, tt=tt):
                    v.wait_ge(s_te.h, fin_tr[tt])
                    src0 = psT[0][:, 0:1024].rearrange("p (c t) -> p c t", t=128)
                    src1 = psT[1][:, 0:1024].rearrange("p (c t) -> p c t", t=128)
                    yr3 = yrowF.rearrange("p (c t) -> p c t", t=128)
                    v.tensor_copy(out=yr3[:, 0:8, :], in_=src0)
                    v.tensor_copy(out=yr3[:, 8:16, :], in_=src1)
                    v.reduce_sum(out=stat_sum[:, 0:1], in_=yrowF[:, :], axis=AX
                                 ).then_inc(s_ve.h, 1)
                fin_ev_ve[tt] = s_ve.inc(1)

                @block.scalar
                def _(sc, tt=tt):
                    sc.wait_ge(s_ve.h, fin_ev_ve[tt])
                    sc.activation(out=junk[:, :], in_=yrowF[:, :], func=AF.Square,
                                  accum_out=stat_sq[:, 0:1]).then_inc(s_se.h, 1)
                yq_se = s_se.inc(1)

                @block.vector
                def _(v, tt=tt, yq_se=yq_se):
                    v.wait_ge(s_se.h, yq_se)
                    v.tensor_scalar(out=stat_m[:, 0:1], in0=stat_sum[:, 0:1],
                                    scalar1=INV_D, scalar2=None, op0=OP.mult)
                    v.tensor_scalar(out=stat_t[:, 0:1], in0=stat_sq[:, 0:1],
                                    scalar1=INV_D, scalar2=None, op0=OP.mult
                                    ).then_inc(s_ve.h, 1)
                yt_ve = s_ve.inc(1)

                @block.scalar
                def _(sc, tt=tt, yt_ve=yt_ve):
                    sc.wait_ge(s_ve.h, yt_ve)
                    sc.activation(out=stat_r[:, 0:1], in_=stat_t[:, 0:1], func=AF.Ln,
                                  bias=eps2_c[:, :])
                    sc.activation(out=stat_r[:, 0:1], in_=stat_r[:, 0:1], func=AF.Exp,
                                  scale=-0.5).then_inc(s_se.h, 1)
                irms_se = s_se.inc(1)

                @block.vector
                def _(v, tt=tt, irms_se=irms_se):
                    v.wait_ge(s_se.h, irms_se)
                    v.tensor_tensor(out=stat_u[:, 0:1], in0=stat_m[:, 0:1],
                                    in1=stat_r[:, 0:1], op=OP.mult)
                    v.tensor_tensor(out=stat_v[:, 0:1], in0=stat_t[:, 0:1],
                                    in1=stat_r[:, 0:1], op=OP.mult)
                    v.tensor_tensor(out=stat_x[:, 0:1], in0=stat_u[:, 0:1],
                                    in1=stat_u[:, 0:1], op=OP.mult)
                    v.tensor_tensor(out=stat_w[:, 0:1], in0=stat_v[:, 0:1],
                                    in1=stat_r[:, 0:1], op=OP.mult)
                    v.tensor_scalar(out=stat_y[:, 0:1], in0=stat_u[:, 0:1],
                                    scalar1=-1.0, scalar2=None, op0=OP.mult)
                    v.tensor_tensor(out=stat_v[:, 0:1], in0=stat_w[:, 0:1],
                                    in1=stat_x[:, 0:1], op=OP.subtract).then_inc(s_ve.h, 1)
                vz_ve = s_ve.inc(1)

                @block.scalar
                def _(sc, tt=tt, vz_ve=vz_ve):
                    sc.wait_ge(s_ve.h, vz_ve)
                    sc.activation(out=stat_t[:, 0:1], in_=stat_v[:, 0:1], func=AF.Ln,
                                  bias=eps1_c[:, :])
                    sc.activation(out=stat_t[:, 0:1], in_=stat_t[:, 0:1], func=AF.Exp,
                                  scale=-0.5).then_inc(s_se.h, 1)
                rz_se = s_se.inc(1)

                @block.sync
                def _(s, g=g, tt=tt):
                    if tt == 0:
                        s.wait_ge(s_sp.h, b3_sp_done)
                        s.wait_ge(s_out.h, s_out.v)
                    else:
                        s.wait_ge(s_out.h, outsp[tt - 1])
                    r0 = g * TG + tt * 128
                    s.dma_start(out=xresF[:, :], in_=sp_x2[r0:r0 + 128, :]
                                ).then_inc(s_x[1].h, 16)
                x2_ld = s_x[1].inc(16)

                @block.vector
                def _(v, g=g, tt=tt, rz_se=rz_se, x2_ld=x2_ld):
                    v.wait_ge(s_se.h, rz_se)
                    v.wait_ge(s_x[1].h, x2_ld)
                    v.tensor_tensor(out=stat_w[:, 0:1], in0=stat_t[:, 0:1],
                                    in1=stat_r[:, 0:1], op=OP.mult)
                    v.tensor_tensor(out=stat_x[:, 0:1], in0=stat_y[:, 0:1],
                                    in1=stat_t[:, 0:1], op=OP.mult)
                    v.tensor_copy(out=gapc[:, :], in_=ones_c[:, :])
                    v.tensor_scalar(out=t1rowF[:, :], in0=yrowF[:, :],
                                    scalar1=stat_w[:, 0:1], scalar2=stat_x[:, 0:1],
                                    op0=OP.mult, op1=OP.add)
                    v.tensor_tensor(out=xresF[:, :], in0=t1rowF[:, :],
                                    in1=xresF[:, :], op=OP.add).then_inc(s_ve.h, 1)
                fin_ve[tt] = s_ve.inc(1)

                @block.sync
                def _(s, g=g, tt=tt):
                    s.wait_ge(s_ve.h, fin_ve[tt])
                    r0 = g * TG + tt * 128
                    s.dma_start(out=y_out[r0:r0 + 128, :], in_=xresF[:, :]
                                ).then_inc(s_out.h, 16)
                outsp[tt] = s_out.inc(16)

            ffn_done_ve = s_ve.v

        if DEBUG_TAPS:
            dbg = {
                "dbg_p": (nc.dram_tensor("dbg_p", [D, TPC], BF16, kind="ExternalOutput").ap(), sp_P),
                "dbg_m": (nc.dram_tensor("dbg_m", [D, TPC], BF16, kind="ExternalOutput").ap(), sp_M),
                "dbg_x2": (nc.dram_tensor("dbg_x2", [TPC, D], F32, kind="ExternalOutput").ap(), sp_x2),
                "dbg_wbf": (nc.dram_tensor("dbg_wbf", [4 * D, D], BF16, kind="ExternalOutput").ap(), wbf),
                "dbg_gu": (nc.dram_tensor("dbg_gu", [D, GU_OSL], BF16, kind="ExternalOutput").ap(), deq_gu_all[0:D, :]),
                "dbg_dn": (nc.dram_tensor("dbg_dn", [I_FF, DN_HSL], BF16, kind="ExternalOutput").ap(), deq_dn_all[0:I_FF, :]),
            }

            @block.gpsimd
            def _(gp):
                gp.wait_ge(s_out.h, s_out.v)
                for name, (o, src_) in dbg.items():
                    gp.dma_start(out=o, in_=src_).then_inc(s_gld.h, 16)
                gp.wait_ge(s_gld.h, s_gld.v + 16 * len(dbg))
            s_gld.inc(16 * len(dbg))

        @block.sync
        def _(s):
            s.wait_ge(s_out.h, s_out.v)

    return nc


_PROGRAM_CACHE = {}


def _get_program():
    if "nc" not in _PROGRAM_CACHE:
        _PROGRAM_CACHE["nc"] = build_program()
    return _PROGRAM_CACHE["nc"]


def kernel(L0_attn_hidden_states, L0_attn_Wr, L0_attn_Wk, L0_attn_Wv, L0_attn_Wo,
           L0_attn_time_decay, L0_attn_time_first, L0_attn_time_mix_k,
           L0_attn_time_mix_v, L0_attn_time_mix_r, L0_ffn_gate_up_packed,
           L0_ffn_gate_up_scales, L0_ffn_down_packed, L0_ffn_down_scales):
    x = np.asarray(L0_attn_hidden_states)
    B, T, _ = x.shape
    x2d = np.ascontiguousarray(x.reshape(B * T, D))

    wrT = np.ascontiguousarray(np.asarray(L0_attn_Wr).T)
    wkT = np.ascontiguousarray(np.asarray(L0_attn_Wk).T)
    wvT = np.ascontiguousarray(np.asarray(L0_attn_Wv).T)
    woT = np.ascontiguousarray(np.asarray(L0_attn_Wo).T)

    def col_layout(vec):
        return np.ascontiguousarray(np.asarray(vec).reshape(CT, 128).T).astype(np.float32)

    tdv = col_layout(np.asarray(L0_attn_time_decay).reshape(-1))
    tfv = col_layout(np.asarray(L0_attn_time_first).reshape(-1))
    tmix = np.ascontiguousarray(np.concatenate(
        [col_layout(np.asarray(v).reshape(-1)) for v in
         (L0_attn_time_mix_k, L0_attn_time_mix_v, L0_attn_time_mix_r)], axis=1))

    idbf = np.eye(128, dtype=ml_dtypes.bfloat16)
    masks = np.zeros((128, 2 * GU_OSL), dtype=np.int32)
    masks[:, 0:GU_OSL] = np.int32(-2147450880)
    masks[:, GU_OSL:] = 0x01C001C0

    gupT = np.ascontiguousarray(np.asarray(L0_ffn_gate_up_packed).T)
    dpT = np.ascontiguousarray(np.asarray(L0_ffn_down_packed).T)
    gu_sc_full = np.asarray(L0_ffn_gate_up_scales)
    d_sc_full = np.asarray(L0_ffn_down_scales)

    in_maps = []
    for c in range(8):
        b, q = divmod(c, 4)
        r0 = b * T + q * TPC
        x_sh = np.ascontiguousarray(x2d[r0:r0 + TPC])
        if q == 0:
            xprev_r = np.zeros((1, D), dtype=np.float32)
        else:
            xprev_r = np.ascontiguousarray(x2d[r0 - 1:r0])
        steps = np.zeros((1, 8), dtype=np.float32)
        mask = np.zeros((1, 8), dtype=np.float32)
        for j in range(8):
            bj, qj = divmod(j, 4)
            if bj == b and qj < q:
                steps[0, j] = TPC * (q - 1 - qj)
                mask[0, j] = 1.0
        osl = slice(GU_OSL * c, GU_OSL * (c + 1))
        hsl = slice(DN_HSL * c, DN_HSL * (c + 1))
        in_maps.append({
            "x_in": x_sh, "xprev": xprev_r, "wrT": wrT, "wkT": wkT, "wvT": wvT,
            "woT": woT, "tmix": tmix, "tdv": tdv, "tfv": tfv,
            "steps8": steps, "mask8": mask, "idbf": idbf, "masks_gu": masks,
            "gup_s": np.ascontiguousarray(gupT[:, osl]),
            "gu_sc": np.ascontiguousarray(
                np.repeat(np.repeat(gu_sc_full[:, osl], 16, axis=0), 2, axis=1)),
            "dp_s": np.ascontiguousarray(dpT[:, hsl]),
            "d_sc": np.ascontiguousarray(
                np.repeat(np.repeat(d_sc_full[:, hsl], 16, axis=0), 2, axis=1)),
        })

    nc = _get_program()
    res = run_bass_kernel_spmd(nc, in_maps, core_ids=list(range(8)))
    out = np.empty((B * T, D), dtype=np.float32)
    for c in range(8):
        b, q = divmod(c, 4)
        r0 = b * T + q * TPC
        out[r0:r0 + TPC] = res.results[c]["y_out"]
    return out.reshape(B, T, D)


# version 14
# speedup vs baseline: 1.0619x; 1.0619x over previous
import numpy as np
import ml_dtypes
import concourse.bass as bass
import concourse.mybir as mybir
from concourse.bass_utils import run_bass_kernel_spmd
from contextlib import ExitStack

F32 = mybir.dt.float32
BF16 = mybir.dt.bfloat16
F16 = mybir.dt.float16
I32 = mybir.dt.int32
I16 = mybir.dt.int16
AF = mybir.ActivationFunctionType
OP = mybir.AluOpType
AX = mybir.AxisListType.X

D = 2048
CT = 16
TPC = 4096
G = 4
TG = 1024
NT = 8
I_FF = 5632
GU_OSL = 1408
DN_HSL = 256
EPS1 = 1e-5
EPS2 = 1e-6
INV_D = 1.0 / D
DEBUG_TAPS = False
DEBUG_TAPS_MID = False




class Sem:
    def __init__(self, h):
        self.h = h
        self.v = 0

    def inc(self, n=1):
        self.v += n
        return self.v


def build_program():
    nc = bass.Bass("TRN2", target_bir_lowering=False)

    def inp(name, shape, dt):
        return nc.dram_tensor(name, shape, dt, kind="ExternalInput").ap()

    x_in = inp("x_in", [TPC, D], F32)
    xprev = inp("xprev", [1, D], F32)
    wrT = inp("wrT", [D, D], F32)
    wkT = inp("wkT", [D, D], F32)
    wvT = inp("wvT", [D, D], F32)
    woT = inp("woT", [D, D], F32)
    tmix = inp("tmix", [128, 3 * CT], F32)
    tdv = inp("tdv", [128, CT], F32)
    tfv = inp("tfv", [128, CT], F32)
    steps8 = inp("steps8", [1, 8], F32)
    mask8 = inp("mask8", [1, 8], F32)
    idbf = inp("idbf", [128, 128], BF16)
    masks_gu = inp("masks_gu", [128, 2 * GU_OSL], I32)
    gup_s = inp("gup_s", [256, GU_OSL], I32)
    gu_sc = inp("gu_sc", [256, 2 * GU_OSL], F16)
    dp_s = inp("dp_s", [704, DN_HSL], I32)
    d_sc = inp("d_sc", [704, 2 * DN_HSL], F16)

    y_out = nc.dram_tensor("y_out", [TPC, D], F32, kind="ExternalOutput").ap()

    def idram(name, shape, dt):
        return nc.dram_tensor(name, shape, dt).ap()

    wbf = idram("wbf", [D, D], BF16)
    wbfT = idram("wbfT", [3 * CT * 128, D], BF16)
    deq_gu_s = idram("deq_gu_s", [D, GU_OSL], BF16)
    deq_dn_s = idram("deq_dn_s", [I_FF, DN_HSL], BF16)
    deq_gu_loc = idram("deq_gu_loc", [GU_OSL, D], BF16)
    deq_gu_all = idram("deq_gu_all", [8 * GU_OSL, D], BF16)
    deq_dn_loc = idram("deq_dn_loc", [DN_HSL, I_FF], BF16)
    deq_dn_all = idram("deq_dn_all", [8 * DN_HSL, I_FF], BF16)
    f_loc = idram("f_loc", [128, CT], F32)
    f_all = idram("f_all", [8 * 128, CT], F32)
    sp_P = idram("sp_P", [D, TPC], BF16)
    sp_M = idram("sp_M", [D, TPC], BF16)
    sp_x2 = idram("sp_x2", [TPC, D], F32)
    if DEBUG_TAPS and DEBUG_TAPS_MID:
        dbg_hs = nc.dram_tensor("dbg_hs", [128, 16400], BF16, kind="ExternalOutput").ap()
        dbg_xr = nc.dram_tensor("dbg_xr", [128, 16384], BF16, kind="ExternalOutput").ap()
        dbg_st = nc.dram_tensor("dbg_st", [128, 640], F32, kind="ExternalOutput").ap()
        dbg_kv = nc.dram_tensor("dbg_kv", [128, 16384], BF16, kind="ExternalOutput").ap()

    es = ExitStack()
    with es:
        sbuf = lambda n, s, d: es.enter_context(nc.sbuf_tensor(n, s, d))
        smallF = sbuf("smallF", [128, 640], F32)
        smallB = sbuf("smallB", [128, 256], BF16)
        _o = [0]

        def sm(w):
            a = _o[0]; _o[0] += w
            return smallF[:, a:a + w]

        ones_c = sm(1)
        eps1_c = sm(1)
        eps2_c = sm(1)
        tdv_sb = sm(CT)
        tfv_sb = sm(CT)
        nexp_sb = sm(CT)
        wdec_sb = sm(CT)
        w1024_sb = sm(CT)
        tmix_sb = sm(3 * CT)
        pball = sm(G * CT)
        steps_b = sm(8)
        mask_b = sm(8)
        fall_sb = sm(128)
        sin_sb = sm(CT)
        sbound = sm(CT)
        c8a = sm(8)
        c8b = sm(8)
        stat_sum = sm(NT)
        stat_sq = sm(NT)
        stat_m = sm(NT)
        stat_r = sm(NT)
        stat_t = sm(NT)
        gapc = sm(1)
        stat_u = sm(NT)
        stat_v = sm(NT)
        stat_w = sm(NT)
        stat_x = sm(NT)
        stat_y = sm(NT)
        _row = sm(144)
        ones_row = _row[0:1, 0:128]
        rows8 = _row[0:1, 128:144]
        prevhs = smallB[:, 0:CT]
        idb_sb = smallB[:, 16:144]
        A_scr = sbuf("A_scr", [128, 9218], F32)
        A_cm = sbuf("A_cm", [128, 16400], BF16)
        A_xM = sbuf("A_xM", [128, 32768], BF16)
        A_big = sbuf("A_big", [128, 26624], BF16)
        A_pan = sbuf("A_pan", [128, 8192], BF16)
        xprev_sb = A_big[0:1, 0:4096].bitcast(F32)
        hprev_sb = A_big[0:1, 4096:6144]
        psF = [es.enter_context(nc.psum_tensor(f"ps{i}", [128, 1024], F32)) for i in range(3)]
        psT = [es.enter_context(nc.psum_tensor(f"psT{i}", [128, 1024], BF16)) for i in range(2)]

        s_ld = Sem(es.enter_context(nc.semaphore("s_ld")))
        s_gld = Sem(es.enter_context(nc.semaphore("s_gld")))
        s_te = Sem(es.enter_context(nc.semaphore("s_te")))
        s_ve = Sem(es.enter_context(nc.semaphore("s_ve")))
        s_se = Sem(es.enter_context(nc.semaphore("s_se")))
        s_sp = Sem(es.enter_context(nc.semaphore("s_sp")))
        s_ag = [Sem(es.enter_context(nc.semaphore(f"s_ag{i}"))) for i in range(3)]
        s_x = [Sem(es.enter_context(nc.semaphore(f"s_x{i}"))) for i in range(2)]
        s_p = [Sem(es.enter_context(nc.semaphore(f"s_p{i}"))) for i in range(2)]
        s_q = [Sem(es.enter_context(nc.semaphore(f"s_q{i}"))) for i in range(2)]
        s_b = [Sem(es.enter_context(nc.semaphore(f"s_b{i}"))) for i in range(2)]
        s_out = Sem(es.enter_context(nc.semaphore("s_out")))
        block = es.enter_context(nc.Block())

        hs3 = A_cm[:, 0:CT * 1025].rearrange("p (c t) -> p c t", t=1025)

        def cmcol(ct, a, b):
            return A_cm[:, ct * 1025 + a: ct * 1025 + b]

        xM_slot = [A_xM[:, 0:CT * 1024], A_xM[:, CT * 1024:2 * CT * 1024]]
        xM3 = [s.rearrange("p (c t) -> p c t", t=1024) for s in xM_slot]
        wkv3 = xM3[0]
        x2ln3 = xM3[1]
        xresF = A_xM[:, 0:4096].bitcast(F32)
        yrowF = A_xM[:, 4096:6144]
        t1rowF = A_xM[:, 6144:10240].bitcast(F32)

        kv3 = A_big[:, 0:CT * 1024].rearrange("p (c t) -> p c t", t=1024)
        ktmp = A_scr[:, 7170:7682].bitcast(BF16)
        hsrow = A_big[:, 16384:18432]
        xrow_a = [A_big[:, 18432 + i * 4096:18432 + (i + 1) * 4096].bitcast(F32) for i in range(2)]
        woT3 = A_big[:, 0:CT * 1024].rearrange("p (c t) -> p c t", t=1024)
        h3 = A_big[:, 0:22 * 1024].rearrange("p (c t) -> p c t", t=1024)

        orow_full = A_cm[:, 0:16384]
        x2row = A_big[:, 16384:20480].bitcast(F32)
        x2lnrow = A_big[:, 20480:22528]
        xresB = A_big[:, 22528:26624].bitcast(F32)
        y23 = A_cm[:, 0:CT * 1024].rearrange("p (c t) -> p c t", t=1024)

        S_buf = A_scr[:, 0:1025]
        wp_buf = A_scr[:, 1025:2050]
        kvf32 = A_scr[:, 2050:3074]
        dmix = A_scr[:, 3074:4098]
        wt_f = dmix
        junk = A_scr[:, 4098:6146]
        zrow = A_scr[:, 6146:7170]
        sgrow = A_scr[:, 7170:7682].bitcast(BF16)
        prowA = [A_scr[:, 7682:8194].bitcast(BF16),
                 A_scr[:, 8194:8706].bitcast(BF16)]
        mrowA = [A_scr[:, 8706:9218].bitcast(BF16),
                 A_scr[:, 1025:1537].bitcast(BF16)]

        panW = [A_pan[:, 0:2048].rearrange("p (c t) -> p c t", t=128),
                A_pan[:, 2048:4096].rearrange("p (c t) -> p c t", t=128)]
        panU = [A_pan[:, 4096:6144].rearrange("p (c t) -> p c t", t=128),
                A_pan[:, 6144:8192].rearrange("p (c t) -> p c t", t=128)]
        panD = [A_pan[:, 0:2816].rearrange("p (c t) -> p c t", t=128),
                A_pan[:, 2816:5632].rearrange("p (c t) -> p c t", t=128)]
        mrowB = [A_pan[:, 0:1024], A_pan[:, 1024:2048]]
        prowB = [A_pan[:, 2048:3072], A_pan[:, 3072:4096]]

        @block.sync
        def _(s):
            for dst, src in [(tmix_sb[:, :], tmix), (tdv_sb[:, :], tdv),
                             (tfv_sb[:, :], tfv), (idb_sb[:, :], idbf),
                             (rows8[:, 0:8], steps8), (rows8[:, 8:16], mask8),
                             (xprev_sb[:, :], xprev)]:
                s.dma_start(out=dst, in_=src).then_inc(s_ld.h, 16)
                s_ld.inc(16)
        ld_consts = s_ld.v

        @block.vector
        def _(v):
            v.memset(ones_c[:, :], 1.0)
            v.memset(eps1_c[:, :], EPS1)
            v.memset(eps2_c[:, :], EPS2)
            v.memset(zrow[:, :], 0.0)
            v.memset(ones_row[:, :], 1.0)
            v.memset(pball[:, 0:CT], 1.0)
            v.memset(sbound[:, :], 0.0).then_inc(s_ve.h, 1)
        s_ve.inc(1)

        @block.scalar
        def _(sc):
            sc.wait_ge(s_ld.h, ld_consts)
            sc.activation(out=nexp_sb[:, :], in_=tdv_sb[:, :], func=AF.Exp
                          ).then_inc(s_se.h, 1)
        s_se.inc(1)

        @block.vector
        def _(v):
            v.wait_ge(s_se.h, s_se.v)
            v.tensor_scalar(out=nexp_sb[:, :], in0=nexp_sb[:, :], scalar1=-1.0,
                            scalar2=None, op0=OP.mult).then_inc(s_ve.h, 1)
        s_ve.inc(1)

        @block.scalar
        def _(sc):
            sc.wait_ge(s_ve.h, s_ve.v)
            sc.activation(out=wdec_sb[:, :], in_=nexp_sb[:, :], func=AF.Exp)
            sc.activation(out=w1024_sb[:, :], in_=nexp_sb[:, :], func=AF.Exp,
                          scale=1024.0).then_inc(s_se.h, 1)
        s_se.inc(1)

        @block.vector
        def _(v):
            v.wait_ge(s_se.h, s_se.v)
            for g in range(1, G):
                v.tensor_tensor(out=pball[:, g * CT:(g + 1) * CT],
                                in0=pball[:, (g - 1) * CT:g * CT],
                                in1=w1024_sb[:, :], op=OP.mult)
                v.tensor_copy(out=gapc[:, :], in_=ones_c[:, :])
            v.tensor_copy(out=gapc[:, :], in_=ones_c[:, :]).then_inc(s_ve.h, 1)
        s_ve.inc(1)
        const_ve = s_ve.v

        @block.tensor
        def _(t):
            t.wait_ge(s_ld.h, ld_consts)
            t.wait_ge(s_ve.h, const_ve)
            t.matmul(psF[0][:, 0:8], ones_row[0:1, :], rows8[0:1, 0:8],
                     start=True, stop=True)
            t.matmul(psF[0][:, 8:16], ones_row[0:1, :], rows8[0:1, 8:16],
                     start=True, stop=True).then_inc(s_te.h, 1)
        s_te.inc(1)

        @block.vector
        def _(v):
            v.wait_ge(s_te.h, s_te.v)
            v.tensor_copy(out=steps_b[:, :], in_=psF[0][:, 0:8])
            v.tensor_copy(out=mask_b[:, :], in_=psF[0][:, 8:16])
            v.reduce_sum(out=stat_sum[0:1, 0:1], in_=xprev_sb[:, :], axis=AX
                         ).then_inc(s_ve.h, 1)
        s_ve.inc(1)
        bc_ve = s_ve.v

        @block.scalar
        def _(sc):
            sc.wait_ge(s_ve.h, bc_ve)
            sc.activation(out=junk[0:1, :], in_=xprev_sb[:, :], func=AF.Square,
                          accum_out=stat_sq[0:1, 0:1]).then_inc(s_se.h, 1)
        s_se.inc(1)

        @block.vector
        def _(v):
            v.wait_ge(s_se.h, s_se.v)
            v.tensor_scalar(out=stat_t[0:1, 0:1], in0=stat_sq[0:1, 0:1],
                            scalar1=INV_D, scalar2=None, op0=OP.mult)
            v.tensor_scalar(out=stat_m[0:1, 0:1], in0=stat_sum[0:1, 0:1],
                            scalar1=INV_D, scalar2=None, op0=OP.mult)
            v.tensor_copy(out=gapc[:, :], in_=ones_c[:, :])
            v.scalar_tensor_tensor(out=stat_r[0:1, 0:1], in0=stat_m[0:1, 0:1],
                                   scalar=stat_m[0:1, 0:1], in1=stat_t[0:1, 0:1],
                                   op0=OP.mult, op1=OP.subtract).then_inc(s_ve.h, 1)
        s_ve.inc(1)

        @block.scalar
        def _(sc):
            sc.wait_ge(s_ve.h, s_ve.v)
            sc.activation(out=stat_r[0:1, 0:1], in_=stat_r[0:1, 0:1], func=AF.Ln,
                          scale=-1.0, bias=eps1_c[0:1, :])
            sc.activation(out=stat_r[0:1, 0:1], in_=stat_r[0:1, 0:1], func=AF.Exp,
                          scale=-0.5).then_inc(s_se.h, 1)
        s_se.inc(1)

        @block.vector
        def _(v):
            v.wait_ge(s_se.h, s_se.v)
            v.tensor_scalar(out=hprev_sb[:, :], in0=xprev_sb[:, :],
                            scalar1=stat_m[0:1, 0:1], scalar2=stat_r[0:1, 0:1],
                            op0=OP.subtract, op1=OP.mult).then_inc(s_ve.h, 1)
        s_ve.inc(1)
        hp_ve = s_ve.v

        @block.tensor
        def _(t):
            t.wait_ge(s_ve.h, hp_ve)
            for ct in range(CT):
                ins = t.transpose(psT[0][:, 2 * ct:2 * ct + 1],
                                  hprev_sb[0:1, ct * 128:(ct + 1) * 128],
                                  idb_sb[0:1, 0:1])
            ins.then_inc(s_te.h, 1)
        s_te.inc(1)

        @block.vector
        def _(v):
            v.wait_ge(s_te.h, s_te.v)
            v.tensor_copy(out=prevhs[:, :], in_=psT[0][:, 0:2 * CT:2]).then_inc(s_ve.h, 1)
        s_ve.inc(1)
        prevhs_ve = s_ve.v

        w32 = [A_big[:, i * 8192:(i + 1) * 8192].bitcast(F32) for i in range(2)]
        wbf_sb = [A_big[:, 16384 + i * 4096:16384 + (i + 1) * 4096] for i in range(2)]
        wsrcs = [wrT, wkT, wvT, woT]
        cast_ld = []
        cast_ve = []
        cast_sp = []
        n_cast = 4 * CT // 2
        for i in range(n_cast):
            m, kt2 = divmod(i, CT // 2)
            kt = 2 * kt2
            sl = i % 2

            @block.sync
            def _(s, m=m, kt=kt, sl=sl, i=i):
                if i < 2:
                    s.wait_ge(s_ve.h, prevhs_ve)
                else:
                    s.wait_ge(s_ve.h, cast_ve[i - 2])
                srcv = wsrcs[m][128 * kt:128 * (kt + 2), :].rearrange(
                    "(k p) o -> p k o", p=128)
                s.dma_start(out=w32[sl][:, :].rearrange("p (k o) -> p k o", o=D),
                            in_=srcv).then_inc(s_x[sl].h, 16)
            cast_ld.append(s_x[sl].inc(16))

            @block.vector
            def _(v, sl=sl, i=i):
                v.wait_ge(s_x[sl].h, cast_ld[i])
                if i >= 2:
                    v.wait_ge(s_q[sl].h, cast_sp[i - 2])
                v.tensor_copy(out=wbf_sb[sl][:, :], in_=w32[sl][:, :]).then_inc(s_ve.h, 1)
            cast_ve.append(s_ve.inc(1))

            @block.scalar
            def _(s, m=m, kt=kt, sl=sl, i=i):
                s.wait_ge(s_ve.h, cast_ve[i])
                if m < 3:
                    for j in range(2):
                        dst = wbfT[m * CT * 128:(m + 1) * CT * 128,
                                   128 * (kt + j):128 * (kt + j + 1)]
                        dst3 = dst.rearrange("(t k) o -> k t o", k=128)
                        src3 = wbf_sb[sl][:, j * D:(j + 1) * D].rearrange(
                            "p (t o) -> p t o", o=128)
                        s.dma_start(out=dst3, in_=src3).then_inc(s_q[sl].h, 16)
                    s_qinc = 32
                else:
                    s.dma_start(out=wbf[128 * kt:128 * (kt + 2), :].rearrange(
                                    "(k p) o -> p k o", p=128),
                                in_=wbf_sb[sl][:, :].rearrange("p (k o) -> p k o", o=D)
                                ).then_inc(s_q[sl].h, 16)
                    s_qinc = 16
            cast_sp.append(s_q[sl].inc(32 if m < 3 else 16))

        u16 = A_big[:, 0:2816].bitcast(I16)
        us16p = [A_big[:, 2816:5632].bitcast(I16), A_big[:, 5632:8448].bitcast(I16)]
        t216p = [A_big[:, 8448:11264].bitcast(I16), A_big[:, 11264:14080].bitcast(I16)]
        mag16 = A_big[:, 14080:16896]
        relu1p_b = A_big[:, 16896:19712]
        mk_sb = A_xM[:, 0:5632].bitcast(I32)
        msign = mk_sb[:, 0:GU_OSL].bitcast(I16)
        mexp = mk_sb[:, GU_OSL:2 * GU_OSL].bitcast(I16)
        sc32b = A_xM[:, 5632:11264].bitcast(F32)
        relu1p = [A_xM[:, 11264:14080], relu1p_b]
        pk2 = [A_xM[:, 14080:16896].bitcast(I32), A_xM[:, 16896:19712].bitcast(I32)]
        sc2 = [A_xM[:, 19712:22528].bitcast(F16), A_xM[:, 22528:25344].bitcast(F16)]
        evb = [A_xM[:, 25344:26752], A_xM[:, 28160:29568]]
        odb = [A_xM[:, 26752:28160], A_xM[:, 29568:30976]]

        @block.sync
        def _(s):
            s.dma_start(out=mk_sb[:, :], in_=masks_gu).then_inc(s_ld.h, 16)
        s_ld.inc(16)

        deq_tiles = []
        for a in range(2):
            deq_tiles.append((gup_s[128 * a:128 * (a + 1), :],
                              gu_sc[128 * a:128 * (a + 1), :], 128, GU_OSL, ("gu", a)))
        for a in range(6):
            npart = 128 if a < 5 else 64
            deq_tiles.append((dp_s[128 * a:128 * a + npart, :],
                              d_sc[128 * a:128 * a + npart, :], npart, DN_HSL, ("dn", a)))
        gu8 = deq_gu_s.rearrange("(k e) o -> k e o", e=8)
        dn8 = deq_dn_s.rearrange("(k e) o -> k e o", e=8)

        tile_ld = [0] * len(deq_tiles)
        tile_ve = [0] * len(deq_tiles)
        pl_store = {}
        n_jp_done = 0
        for ti, (src_pk, src_sc, n_part, w_words, tag) in enumerate(deq_tiles):
            ts_ = ti % 2
            w16 = 2 * w_words

            @block.sync
            def _(s, ti=ti, ts_=ts_, src_pk=src_pk, src_sc=src_sc,
                  n_part=n_part, w_words=w_words, w16=w16):
                if ti >= 2:
                    s.wait_ge(s_ve.h, tile_ve[ti - 2])
                s.dma_start(out=pk2[ts_][0:n_part, 0:w_words], in_=src_pk
                            ).then_inc(s_x[ts_].h, 16)
                s.dma_start(out=sc2[ts_][0:n_part, 0:w16], in_=src_sc
                            ).then_inc(s_x[ts_].h, 16)
            tile_ld[ti] = s_x[ts_].inc(32)

            @block.vector
            def _(v, ti=ti, ts_=ts_, n_part=n_part, w16=w16):
                v.wait_ge(s_x[ts_].h, tile_ld[ti])
                if ti == 0:
                    v.wait_ge(s_ld.h, s_ld.v)
                v.tensor_copy(out=sc32b[0:n_part, 0:w16], in_=sc2[ts_][0:n_part, 0:w16]
                              ).then_inc(s_ve.h, 1)
            s_ve.inc(1)

            kind, a = tag

            def emitA(jp, jpar, n_part=n_part, w_words=w_words, w16=w16, ts_=ts_):
                pk16 = pk2[ts_][0:n_part, 0:w_words].bitcast(I16)

                @block.vector
                def _(v):
                    sl = 12 - 4 * jp
                    if sl > 0:
                        v.tensor_scalar(out=u16[0:n_part, 0:w16], in0=pk16, scalar1=sl,
                                        scalar2=None, op0=OP.logical_shift_left)
                    else:
                        v.tensor_copy(out=u16[0:n_part, 0:w16], in_=pk16)
                    v.tensor_tensor(out=us16p[jpar][0:n_part, 0:w16],
                                    in0=u16[0:n_part, 0:w16],
                                    in1=msign[0:n_part, 0:w16], op=OP.bitwise_and)
                    v.tensor_scalar(out=t216p[jpar][0:n_part, 0:w16],
                                    in0=u16[0:n_part, 0:w16],
                                    scalar1=6, scalar2=None, op0=OP.logical_shift_right)
                    v.tensor_tensor(out=t216p[jpar][0:n_part, 0:w16],
                                    in0=t216p[jpar][0:n_part, 0:w16],
                                    in1=mexp[0:n_part, 0:w16], op=OP.bitwise_and)
                    v.tensor_scalar(out=t216p[jpar][0:n_part, 0:w16],
                                    in0=t216p[jpar][0:n_part, 0:w16],
                                    scalar1=16128.0, scalar2=None, op0=OP.add
                                    ).then_inc(s_ve.h, 1)
                t2_ve = s_ve.inc(1)

                @block.scalar
                def _(sc):
                    sc.wait_ge(s_ve.h, t2_ve)
                    sc.activation(out=relu1p[jpar][0:n_part, 0:w16],
                                  in_=t216p[jpar][0:n_part, 0:w16].bitcast(BF16),
                                  func=AF.Relu, scale=-1.0,
                                  bias=ones_c[0:n_part, :]).then_inc(s_se.h, 1)
                return s_se.inc(1)

            def emitB(jp, jpar, r_se, n_part=n_part, w_words=w_words, w16=w16,
                      kind=kind, a=a):
                @block.vector
                def _(v):
                    v.wait_ge(s_se.h, r_se)
                    if (jpar, "st") in pl_store and pl_store[(jpar, "st")] > 0:
                        v.wait_ge(s_q[jpar].h, pl_store[(jpar, "st")])
                    v.tensor_tensor(out=mag16[0:n_part, 0:w16],
                                    in0=t216p[jpar][0:n_part, 0:w16].bitcast(BF16),
                                    in1=relu1p[jpar][0:n_part, 0:w16], op=OP.subtract)
                    v.tensor_tensor(out=mag16[0:n_part, 0:w16].bitcast(I16),
                                    in0=mag16[0:n_part, 0:w16].bitcast(I16),
                                    in1=us16p[jpar][0:n_part, 0:w16], op=OP.bitwise_or)
                    v.tensor_tensor(out=evb[jpar][0:n_part, 0:w_words],
                                    in0=mag16[0:n_part, 0:w16][:, 0::2],
                                    in1=sc32b[0:n_part, 0:w16][:, 0::2], op=OP.mult)
                    v.tensor_tensor(out=odb[jpar][0:n_part, 0:w_words],
                                    in0=mag16[0:n_part, 0:w16][:, 1::2],
                                    in1=sc32b[0:n_part, 0:w16][:, 1::2],
                                    op=OP.mult).then_inc(s_ve.h, 1)
                pl_ve = s_ve.inc(1)

                @block.sync
                def _(s):
                    s.wait_ge(s_ve.h, pl_ve)
                    base = gu8 if kind == "gu" else dn8
                    for h, buf in ((0, evb[jpar]), (1, odb[jpar])):
                        dst = base[128 * a:128 * a + n_part, 4 * h + jp, :]
                        s.dma_start(out=dst, in_=buf[0:n_part, 0:w_words]
                                    ).then_inc(s_q[jpar].h, 16)
                pl_store[(jpar, "st")] = s_q[jpar].inc(32)

            rses = {}
            jpars = {}
            for jp in range(4):
                jpars[jp] = n_jp_done % 2
                n_jp_done += 1
            rses[0] = emitA(0, jpars[0])
            rses[1] = emitA(1, jpars[1])
            emitB(0, jpars[0], rses[0])
            rses[2] = emitA(2, jpars[2])
            emitB(1, jpars[1], rses[1])
            rses[3] = emitA(3, jpars[3])
            emitB(2, jpars[2], rses[2])
            emitB(3, jpars[3], rses[3])
            tile_ve[ti] = s_ve.v

        gu_v = deq_gu_s.rearrange("(t x) (l o) -> x t l o", x=128, o=128)
        dn_v = deq_dn_s.rearrange("(t x) (l o) -> x t l o", x=128, o=128)

        @block.sync
        def _(s):
            s.wait_ge(s_q[0].h, s_q[0].v)
            s.wait_ge(s_q[1].h, s_q[1].v)
            for l in range(11):
                dst = deq_gu_loc[l * 128:(l + 1) * 128, :].rearrange(
                    "x (t o) -> x t o", o=128)
                s.dma_start(out=dst, in_=gu_v[:, :, l, :]).then_inc(s_sp.h, 16)
            for l in range(2):
                for th in range(2):
                    dst = deq_dn_loc[l * 128:(l + 1) * 128,
                                     2816 * th:2816 * (th + 1)].rearrange(
                        "x (t o) -> x t o", o=128)
                    s.dma_start(out=dst, in_=dn_v[:, 22 * th:22 * (th + 1), l, :]
                                ).then_inc(s_sp.h, 16)
        s_sp.inc(16 * 15)
        deq_sp_done = s_sp.v
        deq_ve_done = s_ve.v

        @block.gpsimd
        def _(gp):
            gp.wait_ge(s_sp.h, deq_sp_done)
            gp.collective_compute("AllGather", OP.bypass,
                                  replica_groups=[list(range(8))],
                                  ins=[deq_gu_loc], outs=[deq_gu_all]
                                  ).then_inc(s_ag[0].h, 1)
            gp.collective_compute("AllGather", OP.bypass,
                                  replica_groups=[list(range(8))],
                                  ins=[deq_dn_loc], outs=[deq_dn_all]
                                  ).then_inc(s_ag[1].h, 1)
        s_ag[0].inc(1)
        s_ag[1].inc(1)

        TM_OFF = {0: 2 * CT, 1: 0, 2: CT}
        WB_OFF = {0: 0, 1: 1, 2: 2}

        ffn_done_ve = 0

        def new_a1_markers():
            return {"tau_ld": [0] * NT, "tau_hs": [0] * NT, "tau_tr0": [0] * NT,
                    "tau_tr1": [0] * NT, "tau_ev_v": [0] * NT, "tau_ev_s": [0] * NT}

        def emit_boundary(g):
            @block.vector
            def _(v, g=g):
                v.wait_ge(s_se.h, s_se.v)
                if g == 0:
                    v.wait_ge(s_ve.h, prevhs_ve)
                    v.wait_ge(s_ve.h, deq_ve_done)
                    v.tensor_copy(out=hs3[:, :, 0], in_=prevhs[:, :]).then_inc(s_ve.h, 1)
                else:
                    v.tensor_copy(out=hs3[:, :, 0], in_=hs3[:, :, 1024]).then_inc(s_ve.h, 1)
            s_ve.inc(1)

        def emit_a1_tau(g, tau, M, carry):
            sl = tau % 2

            def _ld(e, g=g, tau=tau, sl=sl, M=M, carry=carry):
                if g == 0 and tau < 2:
                    e.wait_ge(s_ve.h, deq_ve_done)
                if tau >= 2:
                    e.wait_ge(s_ve.h, M["tau_hs"][tau - 2])
                elif carry is not None:
                    e.wait_ge(s_ve.h, carry["tau_hs"][NT - 2 + tau])
                r0 = g * TG + tau * 128
                e.dma_start(out=xrow_a[sl][:, :], in_=x_in[r0:r0 + 128, :]
                            ).then_inc(s_x[sl].h, 16)
            if g == 0:
                block.sync(_ld)
            else:
                block.gpsimd(_ld)
            M["tau_ld"][tau] = s_x[sl].inc(16)

            @block.vector
            def _(v, tau=tau, sl=sl, M=M):
                v.wait_ge(s_x[sl].h, M["tau_ld"][tau])
                v.reduce_sum(out=stat_sum[:, tau:tau + 1], in_=xrow_a[sl][:, :],
                             axis=AX).then_inc(s_ve.h, 1)
            s_ve.inc(1)

            @block.scalar
            def _(sc, tau=tau, sl=sl, M=M):
                sc.wait_ge(s_x[sl].h, M["tau_ld"][tau])
                sc.activation(out=junk[:, :], in_=xrow_a[sl][:, :], func=AF.Square,
                              accum_out=stat_sq[:, tau:tau + 1]).then_inc(s_se.h, 1)
            s_se.inc(1)
            sq_se = s_se.v

            @block.vector
            def _(v, tau=tau, sq_se=sq_se):
                v.wait_ge(s_se.h, sq_se)
                v.tensor_scalar(out=stat_t[:, tau:tau + 1], in0=stat_sq[:, tau:tau + 1],
                                scalar1=INV_D, scalar2=None, op0=OP.mult)
                v.tensor_scalar(out=stat_m[:, tau:tau + 1], in0=stat_sum[:, tau:tau + 1],
                                scalar1=INV_D, scalar2=None, op0=OP.mult)
                v.tensor_copy(out=gapc[:, :], in_=ones_c[:, :])
                v.scalar_tensor_tensor(out=stat_r[:, tau:tau + 1],
                                       in0=stat_m[:, tau:tau + 1],
                                       scalar=stat_m[:, tau:tau + 1],
                                       in1=stat_t[:, tau:tau + 1],
                                       op0=OP.mult, op1=OP.subtract
                                       ).then_inc(s_ve.h, 1)
            s_ve.inc(1)
            var_ve = s_ve.v

            @block.scalar
            def _(sc, tau=tau, var_ve=var_ve):
                sc.wait_ge(s_ve.h, var_ve)
                sc.activation(out=stat_r[:, tau:tau + 1], in_=stat_r[:, tau:tau + 1],
                              func=AF.Ln, scale=-1.0, bias=eps1_c[:, :])
                sc.activation(out=stat_r[:, tau:tau + 1], in_=stat_r[:, tau:tau + 1],
                              func=AF.Exp, scale=-0.5).then_inc(s_se.h, 1)
            s_se.inc(1)
            rstd_se = s_se.v

            @block.vector
            def _(v, tau=tau, sl=sl, M=M, carry=carry, rstd_se=rstd_se):
                v.wait_ge(s_se.h, rstd_se)
                if tau >= 1:
                    v.wait_ge(s_te.h, M["tau_tr1"][tau - 1])
                elif carry is not None:
                    v.wait_ge(s_te.h, carry["tau_tr1"][NT - 1])
                v.tensor_scalar(out=hsrow[:, :], in0=xrow_a[sl][:, :],
                                scalar1=stat_m[:, tau:tau + 1],
                                scalar2=stat_r[:, tau:tau + 1],
                                op0=OP.subtract, op1=OP.mult).then_inc(s_ve.h, 1)
            M["tau_hs"][tau] = s_ve.inc(1)

        def emit_a1_tr(g, tau, M, carry):
            @block.tensor
            def _(t, tau=tau, M=M, carry=carry):
                t.wait_ge(s_ve.h, M["tau_hs"][tau])
                if tau >= 1:
                    t.wait_ge(s_ve.h, M["tau_ev_v"][tau - 1])
                    t.wait_ge(s_se.h, M["tau_ev_s"][tau - 1])
                elif carry is not None:
                    t.wait_ge(s_ve.h, carry["tau_ev_v"][NT - 1])
                    t.wait_ge(s_se.h, carry["tau_ev_s"][NT - 1])
                for i in range(8):
                    ins = t.transpose(psT[0][:, i * 128:(i + 1) * 128],
                                      hsrow[:, i * 128:(i + 1) * 128], idb_sb[:, :])
                ins.then_inc(s_te.h, 1)
                s_te.inc(1)
                for i in range(8):
                    ins = t.transpose(psT[1][:, i * 128:(i + 1) * 128],
                                      hsrow[:, (8 + i) * 128:(9 + i) * 128],
                                      idb_sb[:, :])
                ins.then_inc(s_te.h, 1)
                s_te.inc(1)
            M["tau_tr1"][tau] = s_te.v
            M["tau_tr0"][tau] = s_te.v - 1

            @block.vector
            def _(v, tau=tau, M=M):
                v.wait_ge(s_te.h, M["tau_tr0"][tau])
                src = psT[0][:, 0:1024].rearrange("p (c t) -> p c t", t=128)
                v.tensor_copy(out=hs3[:, 0:8, 1 + tau * 128:1 + (tau + 1) * 128],
                              in_=src).then_inc(s_ve.h, 1)
            M["tau_ev_v"][tau] = s_ve.inc(1)

            @block.scalar
            def _(sc, tau=tau, M=M):
                sc.wait_ge(s_te.h, M["tau_tr1"][tau])
                src = psT[1][:, 0:1024].rearrange("p (c t) -> p c t", t=128)
                sc.copy(out=hs3[:, 8:16, 1 + tau * 128:1 + (tau + 1) * 128],
                        in_=src).then_inc(s_se.h, 1)
            M["tau_ev_s"][tau] = s_se.inc(1)

        def emit_mix(pj, slot, hs_se_wait, extra_te_wait=None):
            @block.vector
            def _(v, pj=pj, slot=slot, hs_se_wait=hs_se_wait,
                  extra_te_wait=extra_te_wait):
                v.wait_ge(s_se.h, hs_se_wait)
                if extra_te_wait is not None:
                    v.wait_ge(s_te.h, extra_te_wait)
                tmo = TM_OFF[pj]
                for ct in range(CT):
                    hs_c = cmcol(ct, 1, 1025)
                    xx_c = cmcol(ct, 0, 1024)
                    v.tensor_tensor(out=dmix[:, :], in0=hs_c, in1=xx_c, op=OP.subtract)
                    ins = v.scalar_tensor_tensor(
                        out=xM3[slot][:, ct, :], in0=dmix[:, :],
                        scalar=tmix_sb[:, tmo + ct:tmo + ct + 1],
                        in1=xx_c, op0=OP.mult, op1=OP.add)
                ins.then_inc(s_ve.h, 1)
            return s_ve.inc(1)

        M_cur = None
        mix_r_done = mix_k_done = 0
        te_k_prev = None
        sc_sp_prev = None
        for g in range(G):
            rs = g % 2
            if g == 0:
                emit_boundary(0)
                M_cur = new_a1_markers()
                for tau in range(NT):
                    emit_a1_tau(0, tau, M_cur, None)
                    emit_a1_tr(0, tau, M_cur, None)
                mix_r_done = emit_mix(0, rs, M_cur["tau_ev_s"][NT - 1])
                mix_k_done = emit_mix(1, 1 - rs, M_cur["tau_ev_s"][NT - 1])

            r_pan_ld = [0] * CT
            r_te = [0] * CT
            r_se = [0] * CT
            r_sp = [0] * CT
            for ot in range(CT):
                @block.sync
                def _(s, g=g, ot=ot):
                    if ot < 2:
                        if g == 0:
                            s.wait_ge(s_q[0].h, cast_sp[6])
                            s.wait_ge(s_q[1].h, cast_sp[7])
                        else:
                            s.wait_ge(s_te.h, te_k_prev[CT - 2 + ot])
                    else:
                        s.wait_ge(s_te.h, r_te[ot - 2])
                    src = wbfT[(0 * CT + ot) * 128:(0 * CT + ot) * 128 + 128, :]
                    s.dma_start(out=panW[ot % 2][:, :, :],
                                in_=src.rearrange("p (k o) -> p k o", o=128)
                                ).then_inc(s_p[ot % 2].h, 16)
                r_pan_ld[ot] = s_p[ot % 2].inc(16)

                @block.tensor
                def _(t, ot=ot, mix_r_done=mix_r_done, rs=rs):
                    t.wait_ge(s_p[ot % 2].h, r_pan_ld[ot])
                    t.wait_ge(s_ve.h, mix_r_done)
                    if ot >= 3:
                        t.wait_ge(s_se.h, r_se[ot - 3])
                    ps = psF[ot % 3]
                    for kt in range(CT):
                        t.matmul(ps[:, 0:512], panW[ot % 2][:, kt, :],
                                 xM3[rs][:, kt, 0:512], start=(kt == 0), stop=(kt == CT - 1))
                        ins = t.matmul(ps[:, 512:1024], panW[ot % 2][:, kt, :],
                                       xM3[rs][:, kt, 512:1024],
                                       start=(kt == 0), stop=(kt == CT - 1))
                    ins.then_inc(s_te.h, 1)
                r_te[ot] = s_te.inc(1)

                @block.scalar
                def _(sc, ot=ot):
                    sc.wait_ge(s_te.h, r_te[ot])
                    if ot >= 2:
                        sc.wait_ge(s_q[ot % 2].h, r_sp[ot - 2])
                    sc.activation(out=mrowA[ot % 2][:, :], in_=psF[ot % 3][:, :],
                                  func=AF.Sigmoid).then_inc(s_se.h, 1)
                r_se[ot] = s_se.inc(1)

                @block.gpsimd
                def _(gp, g=g, ot=ot):
                    gp.wait_ge(s_se.h, r_se[ot])
                    gp.dma_start(out=sp_M[128 * ot:128 * (ot + 1), g * TG:(g + 1) * TG],
                                 in_=mrowA[ot % 2][:, :]).then_inc(s_q[ot % 2].h, 16)
                r_sp[ot] = s_q[ot % 2].inc(16)

            mix_v_done = emit_mix(2, rs, M_cur["tau_ev_s"][NT - 1],
                                  extra_te_wait=r_te[CT - 1])

            kv_pan_ld = [0] * CT
            te_k = [0] * CT
            te_v = [0] * CT
            se_k = [0] * CT
            ve_kv = [0] * CT
            M_next = new_a1_markers() if g < G - 1 else None
            for ot in range(CT):
                @block.sync
                def _(s, g=g, ot=ot):
                    if ot < 2:
                        s.wait_ge(s_te.h, r_te[CT - 1])
                        if g == 0:
                            s.wait_ge(s_q[0].h, cast_sp[22])
                            s.wait_ge(s_q[1].h, cast_sp[23])
                    else:
                        s.wait_ge(s_te.h, te_v[ot - 2])
                    for woff, dst3 in ((1, panW[ot % 2]), (2, panU[ot % 2])):
                        src = wbfT[(woff * CT + ot) * 128:(woff * CT + ot) * 128 + 128, :]
                        s.dma_start(out=dst3[:, :, :],
                                    in_=src.rearrange("p (k o) -> p k o", o=128)
                                    ).then_inc(s_p[ot % 2].h, 16)
                kv_pan_ld[ot] = s_p[ot % 2].inc(32)

                @block.tensor
                def _(t, ot=ot, mix_k_done=mix_k_done, rs=rs):
                    t.wait_ge(s_p[ot % 2].h, kv_pan_ld[ot])
                    t.wait_ge(s_ve.h, mix_k_done)
                    if ot == 0:
                        t.wait_ge(s_se.h, r_se[CT - 1])
                    if ot >= 2:
                        t.wait_ge(s_ve.h, ve_kv[ot - 2])
                    psk = psF[(2 * ot) % 3]
                    for kt in range(CT):
                        t.matmul(psk[:, 0:512], panW[ot % 2][:, kt, :],
                                 xM3[1 - rs][:, kt, 0:512],
                                 start=(kt == 0), stop=(kt == CT - 1))
                        ins = t.matmul(psk[:, 512:1024], panW[ot % 2][:, kt, :],
                                       xM3[1 - rs][:, kt, 512:1024],
                                       start=(kt == 0), stop=(kt == CT - 1))
                    ins.then_inc(s_te.h, 1)
                te_k[ot] = s_te.inc(1)

                @block.scalar
                def _(sc, ot=ot):
                    sc.wait_ge(s_te.h, te_k[ot])
                    if ot >= 1:
                        sc.wait_ge(s_ve.h, ve_kv[ot - 1])
                    sc.copy(out=ktmp[:, :], in_=psF[(2 * ot) % 3][:, :]).then_inc(s_se.h, 1)
                se_k[ot] = s_se.inc(1)

                @block.tensor
                def _(t, ot=ot, mix_v_done=mix_v_done, rs=rs):
                    t.wait_ge(s_ve.h, mix_v_done)
                    if ot >= 1:
                        t.wait_ge(s_se.h, se_k[ot - 1])
                    psv = psF[(2 * ot + 1) % 3]
                    for kt in range(CT):
                        t.matmul(psv[:, 0:512], panU[ot % 2][:, kt, :],
                                 xM3[rs][:, kt, 0:512], start=(kt == 0), stop=(kt == CT - 1))
                        ins = t.matmul(psv[:, 512:1024], panU[ot % 2][:, kt, :],
                                       xM3[rs][:, kt, 512:1024],
                                       start=(kt == 0), stop=(kt == CT - 1))
                    ins.then_inc(s_te.h, 1)
                te_v[ot] = s_te.inc(1)

                @block.vector
                def _(v, ot=ot):
                    v.wait_ge(s_te.h, te_v[ot])
                    v.wait_ge(s_se.h, se_k[ot])
                    v.tensor_tensor(out=kv3[:, ot, :], in0=psF[(2 * ot + 1) % 3][:, :],
                                    in1=ktmp[:, :], op=OP.mult).then_inc(s_ve.h, 1)
                ve_kv[ot] = s_ve.inc(1)

                if M_next is not None and ot % 2 == 1:
                    tau = (ot - 1) // 2
                    if tau == 0:
                        emit_boundary(g + 1)
                    emit_a1_tau(g + 1, tau, M_next, M_cur)
                    emit_a1_tr(g + 1, tau, M_next, M_cur)

            if M_next is not None:
                mix_r_done = emit_mix(0, 1 - rs, M_next["tau_ev_s"][NT - 1],
                                      extra_te_wait=te_k[CT - 1])
                mix_k_done = emit_mix(1, rs, M_next["tau_ev_s"][NT - 1],
                                      extra_te_wait=te_v[CT - 1])

            sc_sp = [0] * CT
            for ct in range(CT):
                @block.vector
                def _(v, g=g, ct=ct, sc_sp_prev=sc_sp_prev):
                    v.wait_ge(s_ve.h, ve_kv[ct])
                    if ct >= 2:
                        v.wait_ge(s_q[ct % 2].h, sc_sp[ct - 2])
                    elif sc_sp_prev is not None:
                        v.wait_ge(s_q[ct % 2].h, sc_sp_prev[CT - 2 + ct])
                    v.tensor_copy(out=kvf32[:, :], in_=kv3[:, ct, :])
                    v.tensor_scalar(out=wt_f[:, :], in0=zrow[:, :],
                                    scalar1=wdec_sb[:, ct:ct + 1], scalar2=None,
                                    op0=OP.add)
                    v.tensor_copy(out=S_buf[:, 0:1], in_=sbound[:, ct:ct + 1])
                    v.tensor_tensor_scan(out=S_buf[:, 1:1025], data0=wt_f[:, :],
                                         data1=kvf32[:, :],
                                         initial=sbound[:, ct:ct + 1],
                                         op0=OP.mult, op1=OP.add)
                    v.scalar_tensor_tensor(out=prowA[ct % 2][:, :], in0=kvf32[:, :],
                                           scalar=tfv_sb[:, ct:ct + 1],
                                           in1=S_buf[:, 0:1024],
                                           op0=OP.mult, op1=OP.add)
                    v.tensor_copy(out=sbound[:, ct:ct + 1],
                                  in_=S_buf[:, 1024:1025]).then_inc(s_ve.h, 1)
                p_ve = s_ve.inc(1)

                @block.gpsimd
                def _(gp, g=g, ct=ct, p_ve=p_ve):
                    gp.wait_ge(s_ve.h, p_ve)
                    gp.dma_start(out=sp_P[128 * ct:128 * (ct + 1), g * TG:(g + 1) * TG],
                                 in_=prowA[ct % 2][:, :]).then_inc(s_q[ct % 2].h, 16)
                sc_sp[ct] = s_q[ct % 2].inc(16)

            sc_sp_prev = sc_sp
            te_k_prev = te_k
            if M_next is not None:
                M_cur = M_next

        a_done_ve = s_ve.v

        @block.sync
        def _(s):
            s.wait_ge(s_ve.h, a_done_ve)
            s.dma_start(out=f_loc, in_=sbound[:, :]).then_inc(s_sp.h, 16)
        f_sp = s_sp.inc(16)

        @block.gpsimd
        def _(gp):
            gp.wait_ge(s_sp.h, f_sp)
            gp.collective_compute("AllGather", OP.bypass,
                                  replica_groups=[list(range(8))],
                                  ins=[f_loc], outs=[f_all]).then_inc(s_ag[2].h, 1)
            gp.wait_ge(s_ag[2].h, 1)
            for r in range(8):
                gp.dma_start(out=fall_sb[:, 16 * r:16 * (r + 1)],
                             in_=f_all[128 * r:128 * (r + 1), :]).then_inc(s_gld.h, 16)
        s_ag[2].inc(1)
        fall_ld = s_gld.inc(16 * 8)

        for ct in range(CT):
            @block.vector
            def _(v, ct=ct):
                if ct == 0:
                    v.wait_ge(s_gld.h, fall_ld)
                v.tensor_scalar(out=c8a[:, :], in0=steps_b[:, :],
                                scalar1=nexp_sb[:, ct:ct + 1], scalar2=None,
                                op0=OP.mult).then_inc(s_ve.h, 1)
            ca_ve = s_ve.inc(1)

            @block.scalar
            def _(sc, ct=ct, ca_ve=ca_ve):
                sc.wait_ge(s_ve.h, ca_ve)
                sc.activation(out=c8b[:, :], in_=c8a[:, :], func=AF.Exp).then_inc(s_se.h, 1)
            cb_se = s_se.inc(1)

            @block.vector
            def _(v, ct=ct, cb_se=cb_se):
                v.wait_ge(s_se.h, cb_se)
                v.tensor_tensor(out=c8a[:, :], in0=c8b[:, :], in1=mask_b[:, :], op=OP.mult)
                v.tensor_copy(out=gapc[:, :], in_=ones_c[:, :])
                fsl = fall_sb[:, :].rearrange("p (r c) -> p r c", c=CT)[:, :, ct]
                v.tensor_tensor(out=c8b[:, :], in0=c8a[:, :], in1=fsl, op=OP.mult)
                v.tensor_copy(out=gapc[:, :], in_=ones_c[:, :])
                v.reduce_sum(out=sin_sb[:, ct:ct + 1], in_=c8b[:, :], axis=AX
                             ).then_inc(s_ve.h, 1)
            s_ve.inc(1)
        sin_ve = s_ve.v

        for g in range(G):
            b_ld = [0] * CT
            wkv_ve = [0] * CT
            for ct in range(CT):
                @block.sync
                def _(s, g=g, ct=ct):
                    if ct < 2:
                        s.wait_ge(s_ve.h, sin_ve if g == 0 else ffn_done_ve)
                        s.wait_ge(s_sp.h, s_sp.v)
                        s.wait_ge(s_q[0].h, s_q[0].v)
                        s.wait_ge(s_q[1].h, s_q[1].v)
                        s.wait_ge(s_out.h, s_out.v)
                        s.wait_ge(s_te.h, s_te.v)
                    else:
                        s.wait_ge(s_ve.h, wkv_ve[ct - 2])
                    s.dma_start(out=prowB[ct % 2][:, :],
                                in_=sp_P[128 * ct:128 * (ct + 1), g * TG:(g + 1) * TG]
                                ).then_inc(s_b[ct % 2].h, 16)
                    s.dma_start(out=mrowB[ct % 2][:, :],
                                in_=sp_M[128 * ct:128 * (ct + 1), g * TG:(g + 1) * TG]
                                ).then_inc(s_b[ct % 2].h, 16)
                b_ld[ct] = s_b[ct % 2].inc(32)

                @block.vector
                def _(v, g=g, ct=ct):
                    v.wait_ge(s_b[ct % 2].h, b_ld[ct])
                    v.tensor_scalar(out=wt_f[:, :], in0=zrow[:, :],
                                    scalar1=wdec_sb[:, ct:ct + 1], scalar2=None, op0=OP.add)
                    v.tensor_copy(out=wp_buf[:, 0:1],
                                  in_=pball[:, g * CT + ct:g * CT + ct + 1])
                    v.tensor_tensor_scan(out=wp_buf[:, 1:1025], data0=wt_f[:, :],
                                         data1=zrow[:, :],
                                         initial=pball[:, g * CT + ct:g * CT + ct + 1],
                                         op0=OP.mult, op1=OP.add)
                    v.scalar_tensor_tensor(out=kvf32[:, :], in0=wp_buf[:, 0:1024],
                                           scalar=sin_sb[:, ct:ct + 1],
                                           in1=prowB[ct % 2][:, :], op0=OP.mult, op1=OP.add)
                    v.tensor_tensor(out=wkv3[:, ct, :], in0=kvf32[:, :],
                                    in1=mrowB[ct % 2][:, :], op=OP.mult).then_inc(s_ve.h, 1)
                wkv_ve[ct] = s_ve.inc(1)
            wkv_all = s_ve.v

            tt_mm_te = [0] * (2 * NT)
            tt_ev_se = [0] * (2 * NT)
            x2ln_ve = [0] * NT
            b3sp = [0] * NT
            trB_ve = [0] * NT
            trB_se = [0] * NT
            trB_te_last = [0] * NT
            wo_ld = [0, 0]
            for half in range(2):
                @block.sync
                def _(s, g=g, half=half):
                    if half == 0:
                        s.wait_ge(s_ve.h, wkv_all)
                        if g == 0:
                            s.wait_ge(s_q[0].h, cast_sp[30])
                            s.wait_ge(s_q[1].h, cast_sp[31])
                    else:
                        s.wait_ge(s_te.h, tt_mm_te[NT - 1])
                    for kt in range(CT):
                        s.dma_start(out=woT3[:, kt, :],
                                    in_=wbf[128 * kt:128 * (kt + 1),
                                            1024 * half:1024 * (half + 1)]
                                    ).then_inc(s_ld.h, 16)
                wo_ld[half] = s_ld.inc(16 * CT)

                for tt in range(NT):
                    idx = NT * half + tt

                    @block.tensor
                    def _(t, tt=tt, half=half, idx=idx):
                        if tt == 0:
                            t.wait_ge(s_ld.h, wo_ld[half])
                        if idx >= 3:
                            t.wait_ge(s_se.h, tt_ev_se[idx - 3])
                        ps = psF[idx % 3]
                        for ot in range(CT):
                            lhsT = wkv3[:, ot, 128 * tt:128 * (tt + 1)]
                            t.matmul(ps[:, 0:512], lhsT, woT3[:, ot, 0:512],
                                     start=(ot == 0), stop=(ot == CT - 1))
                            ins = t.matmul(ps[:, 512:1024], lhsT, woT3[:, ot, 512:1024],
                                           start=(ot == 0), stop=(ot == CT - 1))
                        ins.then_inc(s_te.h, 1)
                    tt_mm_te[idx] = s_te.inc(1)

                    @block.scalar
                    def _(sc, tt=tt, half=half, idx=idx):
                        sc.wait_ge(s_te.h, tt_mm_te[idx])
                        sc.copy(out=orow_full[:, 2048 * tt + 1024 * half:
                                              2048 * tt + 1024 * (half + 1)],
                                in_=psF[idx % 3][:, :]).then_inc(s_se.h, 1)
                    tt_ev_se[idx] = s_se.inc(1)

            for tt in range(NT):
                orow = orow_full[:, 2048 * tt:2048 * (tt + 1)]

                @block.sync
                def _(s, g=g, tt=tt):
                    if tt >= 1:
                        s.wait_ge(s_ve.h, x2ln_ve[tt - 1])
                    r0 = g * TG + tt * 128
                    s.dma_start(out=xresB[:, :], in_=x_in[r0:r0 + 128, :]
                                ).then_inc(s_x[0].h, 16)
                xres_ld = s_x[0].inc(16)

                @block.vector
                def _(v, tt=tt, orow=orow):
                    v.wait_ge(s_se.h, tt_ev_se[NT + tt])
                    v.reduce_sum(out=stat_sum[:, 0:1], in_=orow, axis=AX
                                 ).then_inc(s_ve.h, 1)
                os_ve = s_ve.inc(1)

                @block.scalar
                def _(sc, tt=tt, os_ve=os_ve, orow=orow):
                    sc.wait_ge(s_ve.h, os_ve)
                    sc.activation(out=junk[:, :], in_=orow, func=AF.Square,
                                  accum_out=stat_sq[:, 0:1]).then_inc(s_se.h, 1)
                osq_se = s_se.inc(1)

                @block.vector
                def _(v, tt=tt, osq_se=osq_se):
                    v.wait_ge(s_se.h, osq_se)
                    v.tensor_scalar(out=stat_t[:, 0:1], in0=stat_sq[:, 0:1],
                                    scalar1=INV_D, scalar2=None, op0=OP.mult)
                    v.tensor_scalar(out=stat_m[:, 0:1], in0=stat_sum[:, 0:1],
                                    scalar1=INV_D, scalar2=None, op0=OP.mult)
                    v.tensor_copy(out=gapc[:, :], in_=ones_c[:, :])
                    v.scalar_tensor_tensor(out=stat_r[:, 0:1], in0=stat_m[:, 0:1],
                                           scalar=stat_m[:, 0:1], in1=stat_t[:, 0:1],
                                           op0=OP.mult, op1=OP.subtract)
                    v.tensor_scalar(out=stat_u[:, 0:1], in0=stat_m[:, 0:1],
                                    scalar1=-1.0, scalar2=None,
                                    op0=OP.mult).then_inc(s_ve.h, 1)
                ov_ve = s_ve.inc(1)

                @block.scalar
                def _(sc, tt=tt, ov_ve=ov_ve):
                    sc.wait_ge(s_ve.h, ov_ve)
                    sc.activation(out=stat_r[:, 0:1], in_=stat_r[:, 0:1], func=AF.Ln,
                                  scale=-1.0, bias=eps1_c[:, :])
                    sc.activation(out=stat_r[:, 0:1], in_=stat_r[:, 0:1], func=AF.Exp,
                                  scale=-0.5).then_inc(s_se.h, 1)
                or_se = s_se.inc(1)

                @block.vector
                def _(v, tt=tt, or_se=or_se, xres_ld=xres_ld, orow=orow):
                    v.wait_ge(s_se.h, or_se)
                    v.wait_ge(s_x[0].h, xres_ld)
                    if tt >= 1:
                        v.wait_ge(s_sp.h, b3sp[tt - 1])
                    v.tensor_tensor(out=stat_v[:, 0:1], in0=stat_u[:, 0:1],
                                    in1=stat_r[:, 0:1], op=OP.mult)
                    v.tensor_copy(out=gapc[:, :], in_=ones_c[:, :])
                    v.tensor_scalar(out=x2row[:, :], in0=orow,
                                    scalar1=stat_r[:, 0:1], scalar2=stat_v[:, 0:1],
                                    op0=OP.mult, op1=OP.add)
                    v.tensor_tensor(out=x2row[:, :], in0=x2row[:, :],
                                    in1=xresB[:, :], op=OP.add)
                    v.reduce_sum(out=stat_sum[:, 0:1], in_=x2row[:, :], axis=AX
                                 ).then_inc(s_ve.h, 1)
                x2_ve = s_ve.inc(1)

                @block.scalar
                def _(sc, tt=tt, x2_ve=x2_ve):
                    sc.wait_ge(s_ve.h, x2_ve)
                    sc.activation(out=junk[:, :], in_=x2row[:, :], func=AF.Square,
                                  accum_out=stat_sq[:, 0:1]).then_inc(s_se.h, 1)
                x2q_se = s_se.inc(1)

                @block.vector
                def _(v, tt=tt, x2q_se=x2q_se):
                    v.wait_ge(s_se.h, x2q_se)
                    v.tensor_scalar(out=stat_t[:, 0:1], in0=stat_sq[:, 0:1],
                                    scalar1=INV_D, scalar2=None, op0=OP.mult)
                    v.tensor_scalar(out=stat_m[:, 0:1], in0=stat_sum[:, 0:1],
                                    scalar1=INV_D, scalar2=None, op0=OP.mult)
                    v.tensor_copy(out=gapc[:, :], in_=ones_c[:, :])
                    v.scalar_tensor_tensor(out=stat_r[:, 0:1], in0=stat_m[:, 0:1],
                                           scalar=stat_m[:, 0:1], in1=stat_t[:, 0:1],
                                           op0=OP.mult, op1=OP.subtract).then_inc(s_ve.h, 1)
                x2v_ve = s_ve.inc(1)

                @block.scalar
                def _(sc, tt=tt, x2v_ve=x2v_ve):
                    sc.wait_ge(s_ve.h, x2v_ve)
                    sc.activation(out=stat_r[:, 0:1], in_=stat_r[:, 0:1], func=AF.Ln,
                                  scale=-1.0, bias=eps1_c[:, :])
                    sc.activation(out=stat_r[:, 0:1], in_=stat_r[:, 0:1], func=AF.Exp,
                                  scale=-0.5).then_inc(s_se.h, 1)
                x2r_se = s_se.inc(1)

                @block.vector
                def _(v, tt=tt, x2r_se=x2r_se):
                    v.wait_ge(s_se.h, x2r_se)
                    if tt >= 1:
                        v.wait_ge(s_te.h, trB_te_last[tt - 1])
                    v.tensor_scalar(out=x2lnrow[:, :], in0=x2row[:, :],
                                    scalar1=stat_m[:, 0:1], scalar2=stat_r[:, 0:1],
                                    op0=OP.subtract, op1=OP.mult).then_inc(s_ve.h, 1)
                x2ln_ve[tt] = s_ve.inc(1)

                @block.sync
                def _(s, g=g, tt=tt):
                    s.wait_ge(s_ve.h, x2ln_ve[tt])
                    r0 = g * TG + tt * 128
                    s.dma_start(out=sp_x2[r0:r0 + 128, :], in_=x2row[:, :]
                                ).then_inc(s_sp.h, 16)
                b3sp[tt] = s_sp.inc(16)

                @block.tensor
                def _(t, tt=tt):
                    t.wait_ge(s_ve.h, x2ln_ve[tt])
                    if tt >= 1:
                        t.wait_ge(s_ve.h, trB_ve[tt - 1])
                        t.wait_ge(s_se.h, trB_se[tt - 1])
                    for i in range(8):
                        ins = t.transpose(psT[0][:, i * 128:(i + 1) * 128],
                                          x2lnrow[:, i * 128:(i + 1) * 128], idb_sb[:, :])
                    ins.then_inc(s_te.h, 1)
                    s_te.inc(1)
                    for i in range(8):
                        ins = t.transpose(psT[1][:, i * 128:(i + 1) * 128],
                                          x2lnrow[:, (8 + i) * 128:(9 + i) * 128],
                                          idb_sb[:, :])
                    ins.then_inc(s_te.h, 1)
                    s_te.inc(1)
                trB_te_last[tt] = s_te.v

                @block.vector
                def _(v, tt=tt):
                    v.wait_ge(s_te.h, trB_te_last[tt] - 1)
                    src = psT[0][:, 0:1024].rearrange("p (c t) -> p c t", t=128)
                    v.tensor_copy(out=x2ln3[:, 0:8, 128 * tt:128 * (tt + 1)],
                                  in_=src).then_inc(s_ve.h, 1)
                trB_ve[tt] = s_ve.inc(1)

                @block.scalar
                def _(sc, tt=tt):
                    sc.wait_ge(s_te.h, trB_te_last[tt])
                    src = psT[1][:, 0:1024].rearrange("p (c t) -> p c t", t=128)
                    sc.copy(out=x2ln3[:, 8:16, 128 * tt:128 * (tt + 1)],
                            in_=src).then_inc(s_se.h, 1)
                trB_se[tt] = s_se.inc(1)

            x2ln_done_ve = s_ve.v
            x2ln_done_se = s_se.v
            b3_sp_done = s_sp.v

            gu_ld = [0] * 44
            gu_te = [0] * 44
            sg_se = [0] * 44
            h_ve = [0] * 44
            dn_te = {}
            y2_ev = {}
            for half in range(2):
                for il in range(22):
                    it = half * 22 + il

                    @block.sync
                    def _(s, g=g, it=it, half=half, il=il):
                        if it < 2:
                            s.wait_ge(s_ve.h, x2ln_done_ve)
                            s.wait_ge(s_se.h, x2ln_done_se)
                            s.wait_ge(s_te.h, s_te.v)
                            if g == 0 and half == 0:
                                s.wait_ge(s_ag[0].h, 1)
                                s.wait_ge(s_ag[1].h, 1)
                        elif half == 1 and il < 2:
                            s.wait_ge(s_te.h, dn_te[(0, CT - 1)])
                            s.wait_ge(s_te.h, gu_te[it - 2])
                        else:
                            s.wait_ge(s_te.h, gu_te[it - 2])
                        for base_o, dst3 in ((128 * it, panW[it % 2]),
                                             (I_FF + 128 * it, panU[it % 2])):
                            blk, c0 = divmod(base_o, GU_OSL)
                            r0 = GU_OSL * blk + c0
                            src = deq_gu_all[r0:r0 + 128, :]
                            s.dma_start(out=dst3[:, :, :],
                                        in_=src.rearrange("p (k o) -> p k o", o=128)
                                        ).then_inc(s_p[it % 2].h, 16)
                    gu_ld[it] = s_p[it % 2].inc(32)

                    @block.tensor
                    def _(t, it=it, half=half, il=il):
                        t.wait_ge(s_p[it % 2].h, gu_ld[it])
                        if it >= 1:
                            t.wait_ge(s_se.h, sg_se[it - 1])
                        if it >= 2:
                            t.wait_ge(s_ve.h, h_ve[it - 2])
                        psg = psF[(2 * it) % 3]
                        psu = psF[(2 * it + 1) % 3]
                        for kt in range(CT):
                            t.matmul(psg[:, 0:512], panW[it % 2][:, kt, :],
                                     x2ln3[:, kt, 0:512], start=(kt == 0), stop=(kt == CT - 1))
                            ins = t.matmul(psg[:, 512:1024], panW[it % 2][:, kt, :],
                                           x2ln3[:, kt, 512:1024],
                                           start=(kt == 0), stop=(kt == CT - 1))
                        ins.then_inc(s_te.h, 1)
                        s_te.inc(1)
                        for kt in range(CT):
                            t.matmul(psu[:, 0:512], panU[it % 2][:, kt, :],
                                     x2ln3[:, kt, 0:512], start=(kt == 0), stop=(kt == CT - 1))
                            ins = t.matmul(psu[:, 512:1024], panU[it % 2][:, kt, :],
                                           x2ln3[:, kt, 512:1024],
                                           start=(kt == 0), stop=(kt == CT - 1))
                        ins.then_inc(s_te.h, 1)
                        s_te.inc(1)
                    gu_te[it] = s_te.v

                    @block.scalar
                    def _(sc, it=it):
                        sc.wait_ge(s_te.h, gu_te[it] - 1)
                        if it >= 1:
                            sc.wait_ge(s_ve.h, h_ve[it - 1])
                        sc.activation(out=sgrow[:, :], in_=psF[(2 * it) % 3][:, :],
                                      func=AF.Silu).then_inc(s_se.h, 1)
                    sg_se[it] = s_se.inc(1)

                    @block.vector
                    def _(v, it=it, il=il):
                        v.wait_ge(s_te.h, gu_te[it])
                        v.wait_ge(s_se.h, sg_se[it])
                        if il == 0:
                            v.wait_ge(s_sp.h, b3_sp_done)
                        v.tensor_tensor(out=h3[:, il, :], in0=psF[(2 * it + 1) % 3][:, :],
                                        in1=sgrow[:, :], op=OP.mult).then_inc(s_ve.h, 1)
                    h_ve[it] = s_ve.inc(1)
                h_half_ve = s_ve.v

                dn_ld = [0] * CT
                for ht in range(CT):
                    @block.sync
                    def _(s, ht=ht, half=half):
                        if ht < 2:
                            s.wait_ge(s_ve.h, h_half_ve)
                        else:
                            s.wait_ge(s_te.h, dn_te[(half, ht - 2)])
                        blk, htl = divmod(ht, 2)
                        r0 = DN_HSL * blk + 128 * htl
                        src = deq_dn_all[r0:r0 + 128, 2816 * half:2816 * (half + 1)]
                        s.dma_start(out=panD[ht % 2][:, :, :],
                                    in_=src.rearrange("p (k o) -> p k o", o=128)
                                    ).then_inc(s_p[ht % 2].h, 16)
                    dn_ld[ht] = s_p[ht % 2].inc(16)

                    @block.tensor
                    def _(t, ht=ht, half=half):
                        t.wait_ge(s_p[ht % 2].h, dn_ld[ht])
                        key = (half, ht - 3)
                        if key in y2_ev:
                            t.wait_ge(y2_ev[key][0], y2_ev[key][1])
                        elif half == 1 and (0, CT - 1) in y2_ev and ht < 3:
                            t.wait_ge(y2_ev[(0, CT - 1)][0], y2_ev[(0, CT - 1)][1])
                        ps = psF[ht % 3]
                        for il in range(22):
                            t.matmul(ps[:, 0:512], panD[ht % 2][:, il, :],
                                     h3[:, il, 0:512], start=(il == 0), stop=(il == 21))
                            ins = t.matmul(ps[:, 512:1024], panD[ht % 2][:, il, :],
                                           h3[:, il, 512:1024],
                                           start=(il == 0), stop=(il == 21))
                        ins.then_inc(s_te.h, 1)
                    dn_te[(half, ht)] = s_te.inc(1)

                    if half == 0:
                        @block.scalar
                        def _(sc, ht=ht):
                            sc.wait_ge(s_te.h, dn_te[(0, ht)])
                            sc.copy(out=y23[:, ht, :], in_=psF[ht % 3][:, :]
                                    ).then_inc(s_se.h, 1)
                        y2_ev[(0, ht)] = (s_se.h, s_se.inc(1))
                    else:
                        @block.vector
                        def _(v, ht=ht):
                            v.wait_ge(s_te.h, dn_te[(1, ht)])
                            v.wait_ge(s_se.h, y2_ev[(0, ht)][1])
                            v.tensor_tensor(out=y23[:, ht, :], in0=psF[ht % 3][:, :],
                                            in1=y23[:, ht, :], op=OP.add).then_inc(s_ve.h, 1)
                        y2_ev[(1, ht)] = (s_ve.h, s_ve.inc(1))

            y2_last_ve = y2_ev[(1, CT - 1)][1]

            fin_ev_ve = [0] * NT
            fin_ve = [0] * NT
            fin_tr = [0] * NT
            outsp = [0] * NT
            for tt in range(NT):
                @block.tensor
                def _(t, tt=tt):
                    if tt == 0:
                        t.wait_ge(s_ve.h, y2_last_ve)
                    else:
                        t.wait_ge(s_ve.h, fin_ev_ve[tt - 1])
                    for i in range(8):
                        ins = t.transpose(psT[0][:, i * 128:(i + 1) * 128],
                                          y23[:, i, 128 * tt:128 * (tt + 1)], idb_sb[:, :])
                    ins.then_inc(s_te.h, 1)
                    s_te.inc(1)
                    for i in range(8):
                        ins = t.transpose(psT[1][:, i * 128:(i + 1) * 128],
                                          y23[:, 8 + i, 128 * tt:128 * (tt + 1)], idb_sb[:, :])
                    ins.then_inc(s_te.h, 1)
                    s_te.inc(1)
                fin_tr[tt] = s_te.v

                @block.vector
                def _(v, tt=tt):
                    v.wait_ge(s_te.h, fin_tr[tt])
                    src0 = psT[0][:, 0:1024].rearrange("p (c t) -> p c t", t=128)
                    src1 = psT[1][:, 0:1024].rearrange("p (c t) -> p c t", t=128)
                    yr3 = yrowF.rearrange("p (c t) -> p c t", t=128)
                    v.tensor_copy(out=yr3[:, 0:8, :], in_=src0)
                    v.tensor_copy(out=yr3[:, 8:16, :], in_=src1)
                    v.reduce_sum(out=stat_sum[:, 0:1], in_=yrowF[:, :], axis=AX
                                 ).then_inc(s_ve.h, 1)
                fin_ev_ve[tt] = s_ve.inc(1)

                @block.scalar
                def _(sc, tt=tt):
                    sc.wait_ge(s_ve.h, fin_ev_ve[tt])
                    sc.activation(out=junk[:, :], in_=yrowF[:, :], func=AF.Square,
                                  accum_out=stat_sq[:, 0:1]).then_inc(s_se.h, 1)
                yq_se = s_se.inc(1)

                @block.vector
                def _(v, tt=tt, yq_se=yq_se):
                    v.wait_ge(s_se.h, yq_se)
                    v.tensor_scalar(out=stat_m[:, 0:1], in0=stat_sum[:, 0:1],
                                    scalar1=INV_D, scalar2=None, op0=OP.mult)
                    v.tensor_scalar(out=stat_t[:, 0:1], in0=stat_sq[:, 0:1],
                                    scalar1=INV_D, scalar2=None, op0=OP.mult
                                    ).then_inc(s_ve.h, 1)
                yt_ve = s_ve.inc(1)

                @block.scalar
                def _(sc, tt=tt, yt_ve=yt_ve):
                    sc.wait_ge(s_ve.h, yt_ve)
                    sc.activation(out=stat_r[:, 0:1], in_=stat_t[:, 0:1], func=AF.Ln,
                                  bias=eps2_c[:, :])
                    sc.activation(out=stat_r[:, 0:1], in_=stat_r[:, 0:1], func=AF.Exp,
                                  scale=-0.5).then_inc(s_se.h, 1)
                irms_se = s_se.inc(1)

                @block.vector
                def _(v, tt=tt, irms_se=irms_se):
                    v.wait_ge(s_se.h, irms_se)
                    v.tensor_tensor(out=stat_u[:, 0:1], in0=stat_m[:, 0:1],
                                    in1=stat_r[:, 0:1], op=OP.mult)
                    v.tensor_tensor(out=stat_v[:, 0:1], in0=stat_t[:, 0:1],
                                    in1=stat_r[:, 0:1], op=OP.mult)
                    v.tensor_tensor(out=stat_x[:, 0:1], in0=stat_u[:, 0:1],
                                    in1=stat_u[:, 0:1], op=OP.mult)
                    v.tensor_tensor(out=stat_w[:, 0:1], in0=stat_v[:, 0:1],
                                    in1=stat_r[:, 0:1], op=OP.mult)
                    v.tensor_scalar(out=stat_y[:, 0:1], in0=stat_u[:, 0:1],
                                    scalar1=-1.0, scalar2=None, op0=OP.mult)
                    v.tensor_tensor(out=stat_v[:, 0:1], in0=stat_w[:, 0:1],
                                    in1=stat_x[:, 0:1], op=OP.subtract).then_inc(s_ve.h, 1)
                vz_ve = s_ve.inc(1)

                @block.scalar
                def _(sc, tt=tt, vz_ve=vz_ve):
                    sc.wait_ge(s_ve.h, vz_ve)
                    sc.activation(out=stat_t[:, 0:1], in_=stat_v[:, 0:1], func=AF.Ln,
                                  bias=eps1_c[:, :])
                    sc.activation(out=stat_t[:, 0:1], in_=stat_t[:, 0:1], func=AF.Exp,
                                  scale=-0.5).then_inc(s_se.h, 1)
                rz_se = s_se.inc(1)

                @block.sync
                def _(s, g=g, tt=tt):
                    if tt == 0:
                        s.wait_ge(s_sp.h, b3_sp_done)
                        s.wait_ge(s_out.h, s_out.v)
                    else:
                        s.wait_ge(s_out.h, outsp[tt - 1])
                    r0 = g * TG + tt * 128
                    s.dma_start(out=xresF[:, :], in_=sp_x2[r0:r0 + 128, :]
                                ).then_inc(s_x[1].h, 16)
                x2_ld = s_x[1].inc(16)

                @block.vector
                def _(v, g=g, tt=tt, rz_se=rz_se, x2_ld=x2_ld):
                    v.wait_ge(s_se.h, rz_se)
                    v.wait_ge(s_x[1].h, x2_ld)
                    v.tensor_tensor(out=stat_w[:, 0:1], in0=stat_t[:, 0:1],
                                    in1=stat_r[:, 0:1], op=OP.mult)
                    v.tensor_tensor(out=stat_x[:, 0:1], in0=stat_y[:, 0:1],
                                    in1=stat_t[:, 0:1], op=OP.mult)
                    v.tensor_copy(out=gapc[:, :], in_=ones_c[:, :])
                    v.tensor_scalar(out=t1rowF[:, :], in0=yrowF[:, :],
                                    scalar1=stat_w[:, 0:1], scalar2=stat_x[:, 0:1],
                                    op0=OP.mult, op1=OP.add)
                    v.tensor_tensor(out=xresF[:, :], in0=t1rowF[:, :],
                                    in1=xresF[:, :], op=OP.add).then_inc(s_ve.h, 1)
                fin_ve[tt] = s_ve.inc(1)

                @block.sync
                def _(s, g=g, tt=tt):
                    s.wait_ge(s_ve.h, fin_ve[tt])
                    r0 = g * TG + tt * 128
                    s.dma_start(out=y_out[r0:r0 + 128, :], in_=xresF[:, :]
                                ).then_inc(s_out.h, 16)
                outsp[tt] = s_out.inc(16)

            ffn_done_ve = s_ve.v

        if DEBUG_TAPS:
            dbg = {
                "dbg_p": (nc.dram_tensor("dbg_p", [D, TPC], BF16, kind="ExternalOutput").ap(), sp_P),
                "dbg_m": (nc.dram_tensor("dbg_m", [D, TPC], BF16, kind="ExternalOutput").ap(), sp_M),
                "dbg_x2": (nc.dram_tensor("dbg_x2", [TPC, D], F32, kind="ExternalOutput").ap(), sp_x2),
                "dbg_wbf": (nc.dram_tensor("dbg_wbf", [4 * D, D], BF16, kind="ExternalOutput").ap(), wbf),
                "dbg_gu": (nc.dram_tensor("dbg_gu", [D, GU_OSL], BF16, kind="ExternalOutput").ap(), deq_gu_all[0:D, :]),
                "dbg_dn": (nc.dram_tensor("dbg_dn", [I_FF, DN_HSL], BF16, kind="ExternalOutput").ap(), deq_dn_all[0:I_FF, :]),
            }

            @block.gpsimd
            def _(gp):
                gp.wait_ge(s_out.h, s_out.v)
                for name, (o, src_) in dbg.items():
                    gp.dma_start(out=o, in_=src_).then_inc(s_gld.h, 16)
                gp.wait_ge(s_gld.h, s_gld.v + 16 * len(dbg))
            s_gld.inc(16 * len(dbg))

        @block.sync
        def _(s):
            s.wait_ge(s_out.h, s_out.v)

    return nc


_PROGRAM_CACHE = {}


def _get_program():
    if "nc" not in _PROGRAM_CACHE:
        _PROGRAM_CACHE["nc"] = build_program()
    return _PROGRAM_CACHE["nc"]


def kernel(L0_attn_hidden_states, L0_attn_Wr, L0_attn_Wk, L0_attn_Wv, L0_attn_Wo,
           L0_attn_time_decay, L0_attn_time_first, L0_attn_time_mix_k,
           L0_attn_time_mix_v, L0_attn_time_mix_r, L0_ffn_gate_up_packed,
           L0_ffn_gate_up_scales, L0_ffn_down_packed, L0_ffn_down_scales):
    x = np.asarray(L0_attn_hidden_states)
    B, T, _ = x.shape
    x2d = np.ascontiguousarray(x.reshape(B * T, D))

    wrT = np.ascontiguousarray(np.asarray(L0_attn_Wr).T)
    wkT = np.ascontiguousarray(np.asarray(L0_attn_Wk).T)
    wvT = np.ascontiguousarray(np.asarray(L0_attn_Wv).T)
    woT = np.ascontiguousarray(np.asarray(L0_attn_Wo).T)

    def col_layout(vec):
        return np.ascontiguousarray(np.asarray(vec).reshape(CT, 128).T).astype(np.float32)

    tdv = col_layout(np.asarray(L0_attn_time_decay).reshape(-1))
    tfv = col_layout(np.asarray(L0_attn_time_first).reshape(-1))
    tmix = np.ascontiguousarray(np.concatenate(
        [col_layout(np.asarray(v).reshape(-1)) for v in
         (L0_attn_time_mix_k, L0_attn_time_mix_v, L0_attn_time_mix_r)], axis=1))

    idbf = np.eye(128, dtype=ml_dtypes.bfloat16)
    masks = np.zeros((128, 2 * GU_OSL), dtype=np.int32)
    masks[:, 0:GU_OSL] = np.int32(-2147450880)
    masks[:, GU_OSL:] = 0x01C001C0

    gupT = np.ascontiguousarray(np.asarray(L0_ffn_gate_up_packed).T)
    dpT = np.ascontiguousarray(np.asarray(L0_ffn_down_packed).T)
    gu_sc_full = np.asarray(L0_ffn_gate_up_scales)
    d_sc_full = np.asarray(L0_ffn_down_scales)

    in_maps = []
    for c in range(8):
        b, q = divmod(c, 4)
        r0 = b * T + q * TPC
        x_sh = np.ascontiguousarray(x2d[r0:r0 + TPC])
        if q == 0:
            xprev_r = np.zeros((1, D), dtype=np.float32)
        else:
            xprev_r = np.ascontiguousarray(x2d[r0 - 1:r0])
        steps = np.zeros((1, 8), dtype=np.float32)
        mask = np.zeros((1, 8), dtype=np.float32)
        for j in range(8):
            bj, qj = divmod(j, 4)
            if bj == b and qj < q:
                steps[0, j] = TPC * (q - 1 - qj)
                mask[0, j] = 1.0
        osl = slice(GU_OSL * c, GU_OSL * (c + 1))
        hsl = slice(DN_HSL * c, DN_HSL * (c + 1))
        in_maps.append({
            "x_in": x_sh, "xprev": xprev_r, "wrT": wrT, "wkT": wkT, "wvT": wvT,
            "woT": woT, "tmix": tmix, "tdv": tdv, "tfv": tfv,
            "steps8": steps, "mask8": mask, "idbf": idbf, "masks_gu": masks,
            "gup_s": np.ascontiguousarray(gupT[:, osl]),
            "gu_sc": np.ascontiguousarray(
                np.repeat(np.repeat(gu_sc_full[:, osl], 16, axis=0), 2, axis=1)),
            "dp_s": np.ascontiguousarray(dpT[:, hsl]),
            "d_sc": np.ascontiguousarray(
                np.repeat(np.repeat(d_sc_full[:, hsl], 16, axis=0), 2, axis=1)),
        })

    nc = _get_program()
    res = run_bass_kernel_spmd(nc, in_maps, core_ids=list(range(8)))
    out = np.empty((B * T, D), dtype=np.float32)
    for c in range(8):
        b, q = divmod(c, 4)
        r0 = b * T + q * TPC
        out[r0:r0 + TPC] = res.results[c]["y_out"]
    return out.reshape(B, T, D)
